# version 35
# baseline (speedup 1.0000x reference)
import math
from contextlib import ExitStack

import numpy as np
import ml_dtypes

import concourse.bass as bass
import concourse.mybir as mybir
from concourse.bass_utils import run_bass_kernel_spmd

BF = ml_dtypes.bfloat16
F32 = mybir.dt.float32
BF16 = mybir.dt.bfloat16
AF = mybir.ActivationFunctionType
ALU = mybir.AluOpType
AX = mybir.AxisListType.X

D = 1024
H = 16
DH = 64
E = 8
EH = 2048
NCHLOC = D // 128


class Prog:
    STREAMS = ("pe", "act", "dve", "pool", "sync")

    NDMASEM = 8

    def __init__(self, nc):
        self.nc = nc
        self.ops = []
        self.last_w = {}
        self.readers = {}
        self.sem_last = {}
        self.dma_ord = {}

    EP = "EPOCH"

    def op(self, stream, fn, reads=(), writes=(), dma=False, tag="",
           lowp=False):
        opid = len(self.ops)
        deps = set()
        if self.EP not in writes:
            reads = tuple(reads) + (self.EP,)
        for b in reads:
            w = self.last_w.get(b)
            if w is not None:
                deps.add(w)
            self.readers.setdefault(b, []).append(opid)
        for b in writes:
            w = self.last_w.get(b)
            if w is not None:
                deps.add(w)
            war = {}
            for r in self.readers.get(b, ()):
                if r == opid:
                    continue
                ro = self.ops[r]
                war[self._sem_of(ro)] = max(war.get(self._sem_of(ro), -1), r)
            deps.update(war.values())
            self.readers[b] = []
            self.last_w[b] = opid
        deps.discard(opid)
        o = dict(id=opid, stream=stream, fn=fn, deps=sorted(deps),
                 dma=dma, tag=tag, lowp=lowp)
        if dma:
            k = self.dma_ord.get(stream, 0)
            self.dma_ord[stream] = k + 1
            o["sem"] = f"{stream}_d{k % self.NDMASEM}"
            o["dval"] = 16 * (k // self.NDMASEM + 1)
        self.ops.append(o)
        self.sem_last[self._sem_of(o)] = opid
        return opid

    def barrier(self, fn=None, stream="dve", skip=()):
        deps = {v for k, v in self.sem_last.items()
                if not any(k.startswith(s) for s in skip)}
        opid = len(self.ops)
        self.readers[self.EP] = []
        self.last_w[self.EP] = opid
        deps.discard(opid)
        o = dict(id=opid, stream=stream, fn=fn, deps=sorted(deps),
                 dma=False, tag="barrier")
        self.ops.append(o)
        self.sem_last[self._sem_of(o)] = opid
        return opid

    def dma(self, queue, out, in_, reads=(), writes=(), tag=""):
        assert queue in ("sync", "pool")
        return self.op(queue,
                       lambda e, o=out, i=in_: e.dma_start(out=o, in_=i),
                       reads=reads, writes=writes, dma=True, tag=tag)

    def _sem_of(self, o):
        if "sem" in o:
            return o["sem"]
        s = o["stream"]
        if s in ("pool", "sync"):
            return s + "_c"
        return s

    def emit(self, final_waits=()):
        nc = self.nc
        needed = set()
        for o in self.ops:
            for d in o["deps"]:
                dep = self.ops[d]
                if (dep["stream"] == "pe" and o["stream"] == "pe"
                        and not dep["dma"]):
                    continue
                if (dep["stream"] == o["stream"] and dep["dma"] and o["dma"]):
                    continue
                needed.add(d)
        fin = []
        for stream, buf in final_waits:
            w = self.last_w.get(buf)
            if w is not None:
                needed.add(w)
                fin.append((stream, w))
        cnt = {}
        incval = {}
        for o in self.ops:
            s = self._sem_of(o)
            if o["dma"]:
                cnt[s] = o["dval"]
                incval[o["id"]] = o["dval"]
            elif o["id"] in needed:
                cnt[s] = cnt.get(s, 0) + 1
                incval[o["id"]] = cnt[s]
        waits = {o["id"]: [] for o in self.ops}
        wm = {}
        for o in self.ops:
            ws = {}
            if o["dma"] and o["dval"] > 16:
                s = self._sem_of(o)
                v = o["dval"] - 16
                if v > wm.get((o["stream"], s), 0):
                    ws[s] = v
            for d in o["deps"]:
                dep = self.ops[d]
                if (dep["stream"] == "pe" and o["stream"] == "pe"
                        and not dep["dma"]):
                    continue
                if (dep["stream"] == o["stream"] and dep["dma"] and o["dma"]):
                    continue
                if d not in incval:
                    raise AssertionError(f"dep {d} of {o['id']} has no inc")
                s = self._sem_of(dep)
                v = incval[d]
                if v > wm.get((o["stream"], s), 0):
                    ws[s] = max(ws.get(s, 0), v)
            for s, v in ws.items():
                wm[(o["stream"], s)] = v
                waits[o["id"]].append((s, v))
        semnames = sorted(cnt.keys())
        sems = {s: nc.alloc_semaphore(f"sem_{s}") for s in semnames}
        stream_ops = {s: [o for o in self.ops if o["stream"] == s]
                      for s in self.STREAMS}
        fin_by_stream = {}
        for stream, w in fin:
            o = self.ops[w]
            fin_by_stream.setdefault(stream, []).append(
                (self._sem_of(o), incval[w]))

        def body(eng, stream):
            for o in stream_ops[stream]:
                for s, v in waits[o["id"]]:
                    eng.wait_ge(sems[s], v)
                if o.get("lowp"):
                    with nc.allow_low_precision("bf16 scale factor; tolerance "
                                                "calibrated for bf16 math"):
                        inst = o["fn"](eng)
                else:
                    inst = o["fn"](eng)
                iv = incval.get(o["id"])
                if iv is not None:
                    o_s = self._sem_of(o)
                    inst.then_inc(sems[o_s], 16 if o["dma"] else 1)
            for s, v in fin_by_stream.get(stream, ()):
                eng.wait_ge(sems[s], v)

        with nc.Block() as block:
            block.tensor(lambda e: body(e, "pe"))
            block.scalar(lambda e: body(e, "act"))
            block.vector(lambda e: body(e, "dve"))
            block.gpsimd(lambda e: body(e, "pool"))
            block.sync(lambda e: body(e, "sync"))
        return dict(n_ops=len(self.ops), n_incs=len(incval),
                    sem_finals={s: cnt[s] for s in semnames})


def build_nc(S=2048, NB=2, dbg=None):
    nc = bass.Bass("TRN2", target_bir_lowering=False)
    JT = S // 128
    IW = min(1024, S)
    NI2 = S // IW
    NIH = IW // 512
    NG = S // 512
    GP = [(g, g + 1) if g + 1 < NG else (g,) for g in range(0, NG, 2)]

    x_d = nc.dram_tensor("x", [NB, D, S], F32, kind="ExternalInput").ap()
    wqkvT_d = nc.dram_tensor("wqkvT", [D, 3 * D], F32, kind="ExternalInput").ap()
    woutT_d = nc.dram_tensor("woutT", [D, D], F32, kind="ExternalInput").ap()
    gateT_d = nc.dram_tensor("gateT", [D, E], F32, kind="ExternalInput").ap()
    bb_d = nc.dram_tensor("bb", [1, E], F32, kind="ExternalInput").ap()
    w1T_d = nc.dram_tensor("w1T", [E, D, EH], F32, kind="ExternalInput").ap()
    w2T_d = nc.dram_tensor("w2T", [E, EH, D], F32, kind="ExternalInput").ap()
    out_d = nc.dram_tensor("out", [NB, D, S], F32, kind="ExternalOutput").ap()
    wqkv_s = nc.dram_tensor("wqkv_s", [D, 3 * D], BF16, kind="Internal").ap()
    wout_s = nc.dram_tensor("wout_s", [D, D], BF16, kind="Internal").ap()
    w1_s = nc.dram_tensor("w1_s", [E, D, EH], BF16, kind="Internal").ap()
    w2_s = nc.dram_tensor("w2_s", [E, EH, D], BF16, kind="Internal").ap()
    identf = nc.inline_tensor(np.eye(128, dtype=np.float32), "identf").ap()
    onesr_b = nc.inline_tensor(np.ones((1, 128), dtype=BF), "onesr_b").ap()
    onesc_b = nc.inline_tensor(np.ones((128, 1), dtype=BF), "onesc_b").ap()
    onesr_f = nc.inline_tensor(np.ones((1, 128), dtype=np.float32), "onesr_f").ap()
    ones64_c = nc.inline_tensor(np.ones((128, 64), dtype=BF), "ones64_c").ap()

    off = [((nc.sbuf_base + 63) // 64) * 64]

    def at(name, shape, dtype, base=None):
        nonlocal off
        if base is None:
            base = off[0]
            b = ((base + 63) // 64) * 64
            t = nc.alloc_sbuf_tensor_at(name, shape, dtype, offset=b)
            sz = int(np.prod(shape[1:])) * (2 if dtype == BF16 else 4)
            off[0] = b + sz
            return t
        b = ((base + 63) // 64) * 64
        return nc.alloc_sbuf_tensor_at(name, shape, dtype, offset=b), b

    xbf = at("xbf", [128, NCHLOC * S], BF16)
    gate_b = at("gate_b", [128, NCHLOC * E], BF16)
    bbB = at("bbB", [128, E], F32)
    cbias = at("cbias", [128, NCHLOC], F32)
    id_f = at("id_f", [128, 128], F32)
    ones_r = at("ones_r", [1, 128], BF16)
    ones_c = at("ones_c", [128, 1], BF16)
    ones_rf = at("ones_rf", [1, 128], F32)
    ones64 = at("ones64", [128, 64], BF16)
    strow = at("strow", [1, 3 * IW], F32)
    rowb = at("rowb", [1, 2 * IW], BF16)
    cstat = at("cstat", [1, 64], F32)
    gm = at("gm", [128, 4], F32)
    ab = off[0]
    Kb, o1 = at("Kb", [128, NCHLOC * S], BF16, base=ab)
    Qb, o2 = at("Qb", [128, NCHLOC * S], BF16, base=o1 + NCHLOC * S * 2)
    Vp, o3 = at("Vp", [128, JT * H * 64], BF16, base=o2 + NCHLOC * S * 2)
    conc, o4 = at("conc", [128, NCHLOC * IW], BF16, base=o3 + JT * H * 64 * 2)
    wring, o5 = at("wring", [128, 2 * 4096], BF16, base=o4 + NCHLOC * IW * 2)
    ptmp, o6 = at("ptmp", [128, 3 * IW], BF16, base=o5 + 2 * 4096 * 2)
    dnb, o7 = at("dnb", [64, 2 * IW], BF16, base=o6 + 3 * IW * 2)
    rmB, o8 = at("rmB", [128, 2 * IW], BF16, base=o7 + 2 * IW * 2)
    sqt, o9 = at("sqt", [128, IW], BF16, base=o8 + 2 * IW * 2)
    concU, o10 = at("concU", [64, 2 * IW], BF16, base=o9 + IW * 2)
    bcbr, o11 = at("bcbr", [64, IW], BF16, base=o10 + 2 * IW * 2)
    at_end = o11 + IW * 2
    ws0, f1 = at("ws0", [128, 16384], BF16, base=ab)
    ws1, f2 = at("ws1", [128, 16384], BF16, base=f1 + 16384 * 2)
    ws2, f3 = at("ws2", [128, 16384], BF16, base=f2 + 16384 * 2)
    hp, f4 = at("hp", [128, 16 * 512], BF16, base=f3 + 16384 * 2)
    acc, f5 = at("acc", [128, NCHLOC * 1024], F32, base=f4 + 16 * 512 * 2)
    outf, f6 = at("outf", [128, 2 * 512], F32, base=f5 + NCHLOC * 1024 * 4)
    htmp, f7 = at("htmp", [128, 2 * 512], BF16, base=f6 + 2 * 512 * 4)
    combB, f8 = at("combB", [128, 6 * 512], BF16, base=f7 + 2 * 512 * 2)
    combq, f9 = at("combq", [128, NG * 4 * E], F32, base=f8 + 6 * 512 * 2)
    gtmp, f10 = at("gtmp", [128, 2 * E], F32, base=f9 + NG * 4 * E * 4)
    ff_end = f10 + 2 * E * 4
    hi = max(at_end, ff_end)
    assert hi <= nc.sbuf_top, f"SBUF over budget: {hi} > {nc.sbuf_top}"

    ps = nc.alloc_psum_tensor("ps", [128, 4096], F32)

    def pcol(c, w, p0=0, p1=128):
        assert c + w <= 4096
        return ps[p0:p1, c:c + w]

    def pbanks(c, w):
        return [f"ps{k}" for k in range(c // 512, (c + w + 511) // 512)]

    ones_rf_ap = ones_rf[:, :]
    nc._dbg = dict(xbf=xbf, Kb=Kb, Qb=Qb, Vp=Vp, conc=conc, acc=acc, hp=hp,
                   cbias=cbias, gm=gm, combq=combq, strow=strow, rmB=rmB)
    p = Prog(nc)
    EP = "EPOCH"

    def bump_epoch(skip=()):
        p.barrier(lambda e: e.memset(cstat[0:1, 63:64], 0.0), skip=skip)

    p.dma("sync", id_f[:, :], identf[:, :], writes=("idf",))
    p.dma("sync", ones_r[:, :], onesr_b[:, :], writes=("onesr",))
    p.dma("sync", ones_c[:, :], onesc_b[:, :], writes=("onesc",))
    p.dma("sync", ones_rf[:, :], onesr_f[:, :], writes=("onesrf",))
    p.dma("sync", ones64[:, :], ones64_c[:, :], writes=("ones64",))
    p.dma("pool", wqkv_s[:, 0:128], wqkvT_d[:, 0:128], writes=("wqkv_sq0",))
    p.dma("pool", xbf[:, :].rearrange("p (c s) -> p c s", c=NCHLOC),
          x_d[0].rearrange("(c p) s -> p c s", p=128),
          reads=(EP,), writes=tuple(f"xbf{c}" for c in range(NCHLOC)),
          tag="xload0")
    p.dma("pool", wqkv_s[:, 128:D], wqkvT_d[:, 128:D], writes=("wqkv_sq",))
    p.dma("pool", wqkv_s[:, D:2 * D], wqkvT_d[:, D:2 * D], writes=("wqkv_sk",))
    p.dma("pool", wqkv_s[:, 2 * D:3 * D], wqkvT_d[:, 2 * D:3 * D],
          writes=("wqkv_sv",))
    p.dma("pool", wout_s[:, :], woutT_d[:, :], writes=("wout_s",))
    p.dma("pool", gate_b[:, :].rearrange("p (c e) -> p c e", c=NCHLOC),
          gateT_d.rearrange("(c p) e -> p c e", p=128), writes=("gateb",))
    p.dma("sync", cstat[0:1, 32:32 + E], bb_d[:, :], writes=("bbrow",))
    p.op("pe", lambda e: e.matmul(pcol(3584, E), ones_rf[:, :],
                                  cstat[0:1, 32:32 + E], start=True, stop=True),
         reads=("onesrf", "bbrow"), writes=("ps7",))
    p.op("act", lambda e: e.copy(out=bbB[:, :], in_=pcol(3584, E)),
         reads=("ps7",), writes=("bbB",))

    for b in range(NB):
        xk = tuple(f"xbf{c}" for c in range(NCHLOC))
        if b > 0:
            for c in range(NCHLOC):
                p.dma("pool", xbf[:, c * S:(c + 1) * S],
                      x_d[b, c * 128:(c + 1) * 128, :],
                      reads=(EP,), writes=(xk[c],), tag=f"xload{b}")

        bump_epoch(skip=("pool_d",))

        NQ = max(1, S // 512)
        QW = min(S, 512)
        NI2S = S // IW
        cslots = [[] for _ in range(2 * NCHLOC - 1 + 2 * JT + 1)]

        def cslot(i):
            return cslots[min(i, len(cslots) - 1)]

        csets = [("q", mc) for mc in range(NCHLOC)] + \
                [("k", mc) for mc in range(NCHLOC)]
        for si, (which, mc) in enumerate(csets):
            srcb = Qb if which == "q" else Kb
            skey = f"Qb{mc}" if which == "q" else f"Kb{mc}"
            hb = (si % 2) * S
            hk = tuple(f"cc{c}" for c in
                       range((si % 2) * NI2S, (si % 2 + 1) * NI2S))
            col = mc if which == "q" else NCHLOC + mc
            base_i = mc if which == "q" else NCHLOC + 1 + mc
            cslot(base_i).append(
                ("act", lambda e, srcb=srcb, mc=mc, hb=hb: e.activation(
                    out=conc[:, hb:hb + S], in_=srcb[:, mc * S:(mc + 1) * S],
                    func=AF.Square), (skey,), hk, "csq"))
            for q in range(NQ):
                cslot(base_i + 1).append(
                    ("pe", lambda e, q=q, hb=hb: e.matmul(
                        pcol(2048 + q * 512, QW, 0, 1), ones_c[:, :],
                        conc[:, hb + q * QW: hb + (q + 1) * QW],
                        start=True, stop=True), ("onesc",) + hk,
                     (f"ps{4 + q}",), "csum"))
            cslot(base_i + 1).append(
                ("dve", lambda e, col=col: e.reduce_max(
                    cstat[0:1, col:col + 1], pcol(2048, NQ * QW, 0, 1),
                    axis=AX),
                 tuple(f"ps{4 + q}" for q in range(NQ)),
                 (f"cm{col}",), "cred"))
        ct0 = 2 * NCHLOC + 2
        cslot(ct0).append(
            ("dve", lambda e: e.tensor_mul(
                cstat[0:1, 16:16 + NCHLOC], cstat[0:1, 0:NCHLOC],
                cstat[0:1, NCHLOC:2 * NCHLOC]),
             tuple(f"cm{c}" for c in range(2 * NCHLOC)), ("cprod",), "cmul"))
        cslot(ct0).append(
            ("act", lambda e: e.activation(
                out=cstat[0:1, 24:24 + NCHLOC],
                in_=cstat[0:1, 16:16 + NCHLOC], func=AF.Ln),
             ("cprod",), ("clnr",), "cln"))
        cslot(ct0 + 2).append(
            ("act", lambda e: e.activation(
                out=cstat[0:1, 40:40 + NCHLOC],
                in_=cstat[0:1, 24:24 + NCHLOC], func=AF.Exp, scale=0.5),
             ("clnr",), ("crow",), "cexp"))
        cslot(ct0 + 2).append(
            ("act", lambda e: e.mul(
                out=cstat[0:1, 48:48 + NCHLOC],
                in_=cstat[0:1, 40:40 + NCHLOC], mul=-0.125),
             ("crow",), ("crow2",), "cneg"))
        cslot(ct0 + 4).append(
            ("pe", lambda e: e.matmul(
                pcol(2560, NCHLOC), ones_rf[:, :],
                cstat[0:1, 48:48 + NCHLOC], start=True, stop=True),
             ("onesrf", "crow2"), ("ps5",), "cbc"))
        cslot(ct0 + 4).append(
            ("act", lambda e: e.copy(
                out=cbias[:, 0:NCHLOC], in_=pcol(2560, NCHLOC)),
             ("ps5",), tuple(f"cb{m}" for m in range(NCHLOC)), "cbcp"))

        def fire(i):
            for stream, fn, rds, wrs, tg in cslots[i]:
                p.op(stream, fn, reads=rds, writes=wrs, tag=tg)

        for blk in range(2 * NCHLOC):
            slot = blk % 2
            wsl = wring[:, slot * 4096: slot * 4096 + 1024]
            src = wqkv_s[:, blk * 128:(blk + 1) * 128]
            wkey = ("wqkv_sq0" if blk == 0 else
                    "wqkv_sq" if blk < NCHLOC else "wqkv_sk")
            p.dma("sync", wsl.rearrange("p (c n) -> p c n", c=NCHLOC),
                  src.rearrange("(c p) n -> p c n", p=128),
                  reads=(wkey, EP, f"wr{slot}"), writes=(f"wr{slot}",),
                  tag=f"qkvw{blk}")
            dst = Qb if blk < NCHLOC else Kb
            dk = "Qb" if blk < NCHLOC else "Kb"
            mc = blk % NCHLOC
            for tt in range(S // 512):
                pc = (tt % 4) * 512
                for fc in range(NCHLOC):
                    p.op("pe", lambda e, wsl=wsl, fc=fc, tt=tt, pc=pc:
                         e.matmul(pcol(pc, 512),
                                  wsl[:, fc * 128:(fc + 1) * 128],
                                  xbf[:, fc * S + tt * 512: fc * S + tt * 512 + 512],
                                  start=(fc == 0), stop=(fc == NCHLOC - 1)),
                         reads=(f"wr{slot}", xk[fc]),
                         writes=(f"ps{pc // 512}",), tag="qkvmm")
                p.op("dve", lambda e, dst=dst, mc=mc, tt=tt, pc=pc:
                     e.tensor_copy(dst[:, mc * S + tt * 512: mc * S + tt * 512 + 512],
                                   pcol(pc, 512)),
                     reads=(f"ps{pc // 512}",), writes=(f"{dk}{mc}",),
                     tag="qkvev")
            if blk >= 1:
                fire(blk - 1)
        for db in range(2):
            slot = db % 2
            wsl = wring[:, slot * 4096: slot * 4096 + 4096]
            src = wqkv_s[:, 2 * D + db * 512: 2 * D + (db + 1) * 512]
            p.dma("sync", wsl.rearrange("p (c n) -> p c n", c=NCHLOC),
                  src.rearrange("(c p) n -> p c n", p=128),
                  reads=("wqkv_sv", EP, f"wr{slot}"), writes=(f"wr{slot}",),
                  tag=f"vw{db}")
            for jt in range(JT):
                pc = (jt % 4) * 512
                for fc in range(NCHLOC):
                    p.op("pe", lambda e, wsl=wsl, fc=fc, jt=jt, pc=pc:
                         e.matmul(pcol(pc, 512),
                                  xbf[:, fc * S + jt * 128: fc * S + jt * 128 + 128],
                                  wsl[:, fc * 512:(fc + 1) * 512],
                                  start=(fc == 0), stop=(fc == NCHLOC - 1)),
                         reads=(f"wr{slot}", xk[fc]),
                         writes=(f"ps{pc // 512}",), tag="vmm")
                vv = Vp[:, jt * 1024 + db * 512: jt * 1024 + (db + 1) * 512]
                p.op("dve", lambda e, vv=vv, pc=pc:
                     e.tensor_copy(vv, pcol(pc, 512)),
                     reads=(f"ps{pc // 512}",), writes=(f"Vp{jt}",), tag="vev")
                fire(2 * NCHLOC - 1 + db * JT + jt)
        fire(len(cslots) - 1)

        if dbg == "qkv":
            for mc in range(NCHLOC):
                p.dma("pool", out_d[b, mc * 128:(mc + 1) * 128, 0:S],
                      Qb[:, mc * S:(mc + 1) * S], reads=(f"Qb{mc}",),
                      writes=("st0",), tag="dbgq")
            info = p.emit(final_waits=[("pool", "st0")])
            nc._dbg_handles = dict(Kb=Kb, Vp=Vp)
            return nc, info

        if b == 0:
            for e_ in range(E):
                p.dma("pool", w1_s[e_], w1T_d[e_],
                      reads=("wr0", "wr1"), writes=(f"w1s{e_}",))
                p.dma("pool", w2_s[e_], w2T_d[e_],
                      reads=("wr0", "wr1"), writes=(f"w2s{e_}",))

        dn_first = [True]

        def dn_evac(h):
            sl = (h % 2) * IW
            ex = ("st0", "st1") if dn_first[0] else ()
            dn_first[0] = False
            for ih in range(NIH):
                p.op("dve", lambda e, sl=sl, ih=ih: e.tensor_copy(
                    concU[0:64, sl + ih * 512: sl + ih * 512 + 512],
                    pcol(OB + ih * 512, 512, 0, 64)),
                     reads=(f"ps{(OB + ih * 512) // 512}",) + ex,
                     writes=(f"cu{h % 2}",), tag="cuev")
            for ih in range(NIH):
                p.op("dve", lambda e, sl=sl, ih=ih: e.tensor_copy(
                    dnb[0:64, sl + ih * 512: sl + ih * 512 + 512],
                    pcol(ih * 512, 512, 0, 64)),
                     reads=(f"ps{ih}",), writes=(f"dn{h % 2}",), tag="dncp")

        OB = 3072
        okeys = ("ps6", "ps7")

        def norm_head(h):
            mc = h // 2
            r0 = (h % 2) * 64
            sl = (h % 2) * IW
            p.op("dve", lambda e, sl=sl: e.reciprocal(
                bcbr[0:64, 0:IW], dnb[0:64, sl:sl + IW]),
                 reads=(f"dn{h % 2}",), writes=("bcbr",), tag="drcp",
                 lowp=True)
            p.op("dve", lambda e, mc=mc, r0=r0, sl=sl:
                 e.tensor_mul(conc[r0:r0 + 64, mc * IW:(mc + 1) * IW],
                              concU[0:64, sl:sl + IW], bcbr[0:64, 0:IW]),
                 reads=(f"cu{h % 2}", "bcbr"),
                 writes=(f"cc{mc}",),
                 tag="avnrm")

        pend_ln1 = [None]

        def ln1_tail(i2p):
            iop = i2p * IW
            _ln_smalls_b(p, pcol, IW, strow, rmB, (rowb, ones_r[:, :]), "at", bccol=1024)
            for ec in range(NCHLOC):
                p.op("dve", lambda e, ec=ec, iop=iop: e.tensor_mul(
                    sqt[:, 0:IW], xbf[:, ec * S + iop: ec * S + iop + IW],
                    rmB[:, 0:IW]),
                    reads=(xk[ec], "rmBr"), writes=("sqt",), tag="nrm1")
                p.op("dve", lambda e, ec=ec, iop=iop: e.tensor_sub(
                    xbf[:, ec * S + iop: ec * S + iop + IW], sqt[:, 0:IW],
                    rmB[:, IW:2 * IW]),
                    reads=("sqt", "rmBm", xk[ec]), writes=(xk[ec],),
                    tag="nrm2")

        for i2 in range(NI2):
            io = i2 * IW
            for h in range(H):
                mc = h // 2
                r0 = (h % 2) * 64

                def emit_s(j):
                    scol = 1024 + (j % 2) * 1024
                    for ih in range(NIH):
                        p.op("pe", lambda e, mc=mc, r0=r0, j=j, ih=ih, io=io,
                             scol=scol:
                             e.matmul(pcol(scol + ih * 512, 512),
                                      Kb[r0:r0 + 64,
                                         mc * S + j * 128: mc * S + j * 128 + 128],
                                      Qb[r0:r0 + 64,
                                         mc * S + io + ih * 512:
                                         mc * S + io + ih * 512 + 512],
                                      start=True, stop=True),
                             reads=(f"Kb{mc}", f"Qb{mc}"),
                             writes=(f"ps{(scol + ih * 512) // 512}",),
                             tag="smm")

                def emit_exp(j):
                    scol = 1024 + (j % 2) * 1024
                    pslot = j % 3
                    p.op("act", lambda e, pslot=pslot, mc=mc, scol=scol:
                         e.activation(out=ptmp[:, pslot * IW:(pslot + 1) * IW],
                                      in_=pcol(scol, IW), func=AF.Exp,
                                      bias=cbias[:, mc:mc + 1], scale=0.125),
                         reads=tuple(pbanks(scol, IW)) + (f"cb{mc}",),
                         writes=(f"pt{pslot}",), tag="exp")

                def emit_avden(j):
                    pslot = j % 3
                    for ih in range(NIH):
                        p.op("pe", lambda e, j=j, ih=ih, pslot=pslot, h=h:
                             e.matmul(pcol(OB + ih * 512, 512, 0, 64),
                                      Vp[:, (j * H + h) * 64:
                                         (j * H + h) * 64 + 64],
                                      ptmp[:, pslot * IW + ih * 512:
                                           pslot * IW + ih * 512 + 512],
                                      start=(j == 0), stop=(j == JT - 1)),
                             reads=(f"Vp{j}", f"pt{pslot}"),
                             writes=(f"ps{(OB + ih * 512) // 512}",),
                             tag="avmm")
                        p.op("pe", lambda e, j=j, ih=ih, pslot=pslot:
                             e.matmul(pcol(ih * 512, 512, 0, 64),
                                      ones64[:, :],
                                      ptmp[:, pslot * IW + ih * 512:
                                           pslot * IW + ih * 512 + 512],
                                      start=(j == 0), stop=(j == JT - 1)),
                             reads=("ones64", f"pt{pslot}"),
                             writes=(f"ps{ih}",),
                             tag="dnmm")

                emit_s(0)
                emit_s(1)
                emit_exp(0)
                if h > 0:
                    norm_head(h - 1)
                elif pend_ln1[0] is not None:
                    ln1_tail(pend_ln1[0])
                    pend_ln1[0] = None
                for j in range(2, JT):
                    emit_s(j)
                    emit_exp(j - 1)
                    emit_avden(j - 2)
                emit_exp(JT - 1)
                emit_avden(JT - 2)
                emit_avden(JT - 1)
                dn_evac(h)
            norm_head(H - 1)
            if dbg == "conc":
                for fc in range(NCHLOC):
                    p.dma("pool", out_d[b, fc * 128:(fc + 1) * 128, io:io + IW],
                          conc[:, fc * IW:(fc + 1) * IW], reads=(f"cc{fc}",),
                          writes=("st0",), tag="dbgc")
                continue
            def _pstats(ec):
                stgt = sqt if ec % 2 == 0 else ptmp
                skey = "sqt" if ec % 2 == 0 else "pt0"
                for ih in range(NIH):
                    p.op("pe", lambda e, ec=ec, io=io, ih=ih: e.matmul(
                        pcol(1024 + ih * 512, 512, 0, 1), ones_c[:, :],
                        xbf[:, ec * S + io + ih * 512:
                            ec * S + io + ih * 512 + 512],
                        start=(ec == 0), stop=(ec == NCHLOC - 1)),
                        reads=("onesc", xk[ec]),
                        writes=(f"ps{2 + ih}",), tag="s1mm")
                    p.op("pe", lambda e, ih=ih, ec=ec, stgt=stgt: e.matmul(
                        pcol(2048 + ih * 512, 512, 0, 1), ones_c[:, :],
                        stgt[:, ih * 512: ih * 512 + 512],
                        start=(ec == 0), stop=(ec == NCHLOC - 1)),
                        reads=("onesc", skey),
                        writes=(f"ps{4 + ih}",), tag="s2mm")

            for ec in range(NCHLOC):
                slot = ec % 2
                pb = 0 if ec % 2 == 0 else 3072
                wsl = wring[:, slot * 4096: slot * 4096 + 1024]
                src = wout_s[:, ec * 128:(ec + 1) * 128]
                p.dma("sync", wsl.rearrange("p (c n) -> p c n", c=NCHLOC),
                      src.rearrange("(c p) n -> p c n", p=128),
                      reads=("wout_s", EP, f"wr{slot}"), writes=(f"wr{slot}",),
                      tag="woutw")
                for ih in range(NIH):
                    for fc in range(NCHLOC):
                        p.op("pe", lambda e, wsl=wsl, fc=fc, ih=ih, pb=pb:
                             e.matmul(pcol(pb + ih * 512, 512),
                                      wsl[:, fc * 128:(fc + 1) * 128],
                                      conc[:, fc * IW + ih * 512:
                                           fc * IW + ih * 512 + 512],
                                      start=(fc == 0), stop=(fc == NCHLOC - 1)),
                             reads=(f"wr{slot}", f"cc{fc}"),
                             writes=(f"ps{(pb + ih * 512) // 512}",),
                             tag="pjmm")
                p.op("dve", lambda e, ec=ec, io=io, pb=pb: e.tensor_add(
                    xbf[:, ec * S + io: ec * S + io + IW],
                    xbf[:, ec * S + io: ec * S + io + IW], pcol(pb, IW)),
                    reads=tuple(pbanks(pb, IW)) + (xk[ec],), writes=(xk[ec],),
                    tag="resid1")
                stgt = sqt if ec % 2 == 0 else ptmp
                skey = "sqt" if ec % 2 == 0 else "pt0"
                p.op("act", lambda e, ec=ec, io=io, stgt=stgt: e.activation(
                    out=stgt[:, 0:IW],
                    in_=xbf[:, ec * S + io: ec * S + io + IW],
                    func=AF.Square), reads=(xk[ec],), writes=(skey,),
                    tag="lnsq")
                if ec >= 1:
                    _pstats(ec - 1)
            _pstats(NCHLOC - 1)
            _ln_smalls_a(p, pcol, IW, D, strow, "at", s1c=1024, s2c=2048)
            if dbg == "y1":
                pend_ln1[0] = None
            elif i2 == NI2 - 1:
                ln1_tail(i2)
            else:
                pend_ln1[0] = i2

        if dbg == "conc":
            break
        if dbg in ("y1", "x1"):
            break
        NBLK = NG * 4
        Gt = outf[:, 0:NBLK * E]
        M1 = outf[:, NBLK * E: NBLK * E + NBLK]
        M2 = outf[:, NBLK * E + NBLK: NBLK * E + 2 * NBLK]
        for t in range(NBLK):
            to = t * 128
            for fc in range(NCHLOC):
                p.op("pe", lambda e, fc=fc, to=to, t=t: e.matmul(
                    pcol(t * E, E), xbf[:, fc * S + to: fc * S + to + 128],
                    gate_b[:, fc * E:(fc + 1) * E],
                    start=(fc == 0), stop=(fc == NCHLOC - 1)),
                    reads=(xk[fc], "gateb"), writes=("ps0",),
                    tag="gmm")
        p.op("act", lambda e: e.copy(out=Gt, in_=pcol(0, NBLK * E)),
             reads=("ps0", "st0", "st1"), writes=("gt",), tag="gevac")
        for t in range(NBLK):
            p.op("dve", lambda e, t=t: e.tensor_add(
                Gt[:, t * E:(t + 1) * E], Gt[:, t * E:(t + 1) * E], bbB[:, :]),
                reads=("gt", "bbB"), writes=("gt",), tag="gadd")
        p.op("act", lambda e: e.activation(out=Gt, in_=Gt, func=AF.Sigmoid),
             reads=("gt",), writes=("gt",), tag="gsig")
        cqs = [combq[:, t * E:(t + 1) * E] for t in range(NBLK)]
        cqk = [f"cq{t // 4}{t % 4}" for t in range(NBLK)]
        for t in range(NBLK):
            p.op("dve", lambda e, t=t: e.reduce_max(
                M1[:, t:t + 1], Gt[:, t * E:(t + 1) * E], axis=AX),
                reads=("gt",), writes=("gm1",), tag="gm1")
        for t in range(NBLK):
            p.op("dve", lambda e, t=t: e.tensor_scalar(
                cqs[t], Gt[:, t * E:(t + 1) * E], M1[:, t:t + 1], -1e30,
                ALU.is_ge, ALU.mult),
                reads=("gt", "gm1"), writes=(cqk[t],), tag="gmask")
        for t in range(NBLK):
            p.op("dve", lambda e, t=t: e.tensor_add(
                cqs[t], cqs[t], Gt[:, t * E:(t + 1) * E]),
                reads=(cqk[t], "gt"), writes=(cqk[t],), tag="gmadd")
        for t in range(NBLK):
            p.op("dve", lambda e, t=t: e.reduce_max(
                M2[:, t:t + 1], cqs[t], axis=AX),
                reads=(cqk[t],), writes=("gm2",), tag="gm2")
        for t in range(NBLK):
            p.op("dve", lambda e, t=t: e.tensor_scalar(
                cqs[t], Gt[:, t * E:(t + 1) * E], M2[:, t:t + 1], None,
                ALU.is_ge),
                reads=("gt", "gm2"), writes=(cqk[t],), tag="gmask2")
        p.op("dve", lambda e: e.tensor_add(M1, M1, M2),
             reads=("gm1", "gm2"), writes=("gm1",))
        p.op("dve", lambda e: e.tensor_scalar_add(M1, M1, 1e-6),
             reads=("gm1",), writes=("gm1",))
        p.op("dve", lambda e: e.reciprocal(M1, M1),
             reads=("gm1",), writes=("gm1",))
        for t in range(NBLK):
            p.op("dve", lambda e, t=t: e.scalar_tensor_tensor(
                cqs[t], Gt[:, t * E:(t + 1) * E], M1[:, t:t + 1], cqs[t],
                ALU.mult, ALU.mult),
                reads=("gt", "gm1", cqk[t]), writes=(cqk[t],), tag="gcomb")

        prefetched_e0 = (S == 2048)
        if prefetched_e0:
            assert o1 - NCHLOC * S * 2 == f1 - 16384 * 2 and f1 == o1 \
                and f2 == o2, "ws/Kb/Qb alias layout changed"
            p.dma("sync", ws0[:, :].rearrange("p (c n) -> p c n", c=NCHLOC),
                  w1_s[0].rearrange("(c p) n -> p c n", p=128),
                  reads=("w1s0",),
                  writes=("ws0",) + tuple(f"Kb{m}" for m in range(NCHLOC)),
                  tag="w1pre")
            p.dma("sync", ws1[:, :].rearrange("p (c n) -> p c n", c=16),
                  w2_s[0].rearrange("(c p) n -> p c n", p=128),
                  reads=("w2s0",),
                  writes=("ws1",) + tuple(f"Qb{m}" for m in range(NCHLOC)),
                  tag="w2pre")
        bump_epoch()
        wslots = [ws0, ws1, ws2]
        wskeys = ["ws0", "ws1", "ws2"]
        si = 0
        for gp in GP:
            for e_ in range(E):
                csl = si
                sl1 = wslots[si]
                k1 = wskeys[si]
                sl2 = wslots[(si + 1) % 3]
                k2 = wskeys[(si + 1) % 3]
                si = (si + 2) % 3
                if not (gp is GP[0] and e_ == 0 and prefetched_e0):
                    p.dma("sync",
                          sl1[:, :].rearrange("p (c n) -> p c n", c=NCHLOC),
                          w1_s[e_].rearrange("(c p) n -> p c n", p=128),
                          reads=(f"w1s{e_}", EP, k1), writes=(k1,),
                          tag=f"w1l{e_}")
                    p.dma("sync",
                          sl2[:, :].rearrange("p (c n) -> p c n", c=16),
                          w2_s[e_].rearrange("(c p) n -> p c n", p=128),
                          reads=(f"w2s{e_}", EP, k2), writes=(k2,),
                          tag=f"w2l{e_}")
                for gi, g in enumerate(gp):
                    for tt in range(4):
                        p.op("pe", lambda e, e_=e_, gi=gi, tt=tt, g=g:
                             e.transpose(
                            pcol(1024 + gi * 512 + tt * 128, 128, 0, 1),
                            combq[:, (g * 4 + tt) * E + e_:
                                  (g * 4 + tt) * E + e_ + 1],
                            id_f[:, :]),
                            reads=(f"cq{g}{tt}", "idf"),
                            writes=(f"ps{2 + gi}",), tag="ctr")
                    p.op("act", lambda e, gi=gi: e.copy(
                        out=rowb[0:1, gi * 512:(gi + 1) * 512],
                        in_=pcol(1024 + gi * 512, 512, 0, 1)),
                        reads=(f"ps{2 + gi}",), writes=("rowb_r",),
                        tag="crev")
                    p.op("pe", lambda e, gi=gi: e.matmul(
                        pcol(1024 + gi * 512, 512), ones_r[:, :],
                        rowb[0:1, gi * 512:(gi + 1) * 512],
                        start=True, stop=True),
                        reads=("onesr", "rowb_r"), writes=(f"ps{2 + gi}",),
                        tag="cbcm")
                    p.op("dve", lambda e, gi=gi: e.tensor_copy(
                        combB[:, gi * 512:(gi + 1) * 512],
                        pcol(1024 + gi * 512, 512)),
                        reads=(f"ps{2 + gi}",), writes=(f"cB{gi}",),
                        tag="cbev")
                for gi, g in enumerate(gp):
                    go = g * 512
                    for ehc in range(16):
                        pc = (ehc % 3) * 512
                        for fc in range(NCHLOC):
                            p.op("pe", lambda e, sl1=sl1, fc=fc, ehc=ehc,
                                 go=go, pc=pc: e.matmul(
                                     pcol(pc, 512),
                                     sl1[:, fc * 2048 + ehc * 128:
                                         fc * 2048 + ehc * 128 + 128],
                                     xbf[:, fc * S + go: fc * S + go + 512],
                                     start=(fc == 0), stop=(fc == NCHLOC - 1)),
                                 reads=(k1, xk[fc]),
                                 writes=(f"ps{pc // 512}",), tag="amm")
                        hts = ehc % 2
                        p.op("act", lambda e, hts=hts, pc=pc: e.activation(
                            out=htmp[:, hts * 512:(hts + 1) * 512],
                            in_=pcol(pc, 512), func=AF.Silu),
                            reads=(f"ps{pc // 512}",), writes=(f"ht{hts}",),
                            tag="silu")
                        p.op("dve", lambda e, hts=hts, ehc=ehc, gi=gi:
                             e.tensor_mul(hp[:, ehc * 512:(ehc + 1) * 512],
                                          htmp[:, hts * 512:(hts + 1) * 512],
                                          combB[:, gi * 512:(gi + 1) * 512]),
                             reads=(f"ht{hts}", f"cB{gi}"),
                             writes=(f"hp{ehc}",), tag="hsc")
                    for dh in range(2):
                        base = dh * 2048
                        for ehc in range(16):
                            for dc in range(4):
                                d_ = dh * 4 + dc
                                p.op("pe", lambda e, sl2=sl2, ehc=ehc, d_=d_,
                                     base=base, dc=dc: e.matmul(
                                         pcol(base + dc * 512, 512),
                                         sl2[:, ehc * 1024 + d_ * 128:
                                             ehc * 1024 + d_ * 128 + 128],
                                         hp[:, ehc * 512:(ehc + 1) * 512],
                                         start=(ehc == 0), stop=(ehc == 15)),
                                     reads=(k2, f"hp{ehc}"),
                                     writes=(f"ps{(base + dc * 512) // 512}",),
                                     tag="bmm")
                        for dc in range(4):
                            d_ = dh * 4 + dc
                            ackey = f"ac{gi}{d_}"
                            asl = acc[:, (gi * NCHLOC + d_) * 512:
                                      (gi * NCHLOC + d_ + 1) * 512]
                            if e_ == 0:
                                p.op("dve", lambda e, asl=asl, base=base,
                                     dc=dc: e.tensor_copy(
                                         asl, pcol(base + dc * 512, 512)),
                                     reads=(f"ps{(base + dc * 512) // 512}",),
                                     writes=(ackey,), tag="acc0")
                            else:
                                p.op("dve", lambda e, asl=asl, base=base,
                                     dc=dc: e.tensor_add(
                                         asl, asl, pcol(base + dc * 512, 512)),
                                     reads=(f"ps{(base + dc * 512) // 512}",
                                            ackey),
                                     writes=(ackey,), tag="acc")
            for gi, g in enumerate(gp):
                go = g * 512
                for ec in range(NCHLOC):
                    ackey = f"ac{gi}{ec}"
                    asl = acc[:, (gi * NCHLOC + ec) * 512:
                              (gi * NCHLOC + ec + 1) * 512]
                    hsl = hp[:, (gi * NCHLOC + ec) * 512:
                             (gi * NCHLOC + ec + 1) * 512]
                    hkey = f"hp{gi * NCHLOC + ec}"
                    p.op("dve", lambda e, asl=asl, hsl=hsl, ec=ec, go=go:
                         e.tensor_add(hsl,
                                      xbf[:, ec * S + go: ec * S + go + 512],
                                      asl),
                         reads=(ackey, xk[ec]), writes=(hkey,), tag="y2")
                    p.op("act", lambda e, hsl=hsl: e.activation(
                        out=htmp[:, 0:512], in_=hsl, func=AF.Square),
                        reads=(hkey,), writes=("ht0",), tag="l2sq")
                    p.op("pe", lambda e, ec=ec, gi=gi, hsl=hsl: e.matmul(
                        pcol(2048 + gi * 512, 512, 0, 1), ones_c[:, :], hsl,
                        start=(ec == 0), stop=(ec == NCHLOC - 1)),
                        reads=("onesc", hkey), writes=(f"ps{4 + gi}",),
                        tag="t1mm")
                    p.op("pe", lambda e, ec=ec, gi=gi: e.matmul(
                        pcol(3072 + gi * 512, 512, 0, 1), ones_c[:, :],
                        htmp[:, 0:512],
                        start=(ec == 0), stop=(ec == NCHLOC - 1)),
                        reads=("onesc", "ht0"), writes=(f"ps{6 + gi}",),
                        tag="t2mm")
            WW = 512 * len(gp)
            _ln_smalls(p, pcol, WW, D, strow, combB,
                       (rowb, ones_r[:, :]), "ff",
                       s1c=2048, s2c=3072, bccol=2048,
                       roff=1024, moff=2048, rkey="cB2", mkey="cB3")
            for gi, g in enumerate(gp):
                go = g * 512
                for ec in range(NCHLOC):
                    osl = ec % 2
                    hsl = hp[:, (gi * NCHLOC + ec) * 512:
                             (gi * NCHLOC + ec + 1) * 512]
                    hkey = f"hp{gi * NCHLOC + ec}"
                    p.op("dve", lambda e, hsl=hsl, gi=gi: e.tensor_mul(
                        htmp[:, 512:1024], hsl,
                        combB[:, 1024 + gi * 512: 1024 + (gi + 1) * 512]),
                        reads=(hkey, "cB2"), writes=("ht1",), tag="o1")
                    p.op("dve", lambda e, osl=osl, gi=gi: e.tensor_sub(
                        outf[:, osl * 512:(osl + 1) * 512], htmp[:, 512:1024],
                        combB[:, 2048 + gi * 512: 2048 + (gi + 1) * 512]),
                        reads=("ht1", "cB3"), writes=(f"of{osl}",), tag="o2")
                    p.dma("pool",
                          out_d[b, ec * 128:(ec + 1) * 128, go:go + 512],
                          outf[:, osl * 512:(osl + 1) * 512],
                          reads=(f"of{osl}",), writes=(f"st{osl}",),
                          tag="store")

    if dbg in ("conc", "y1", "x1"):
        if dbg in ("y1", "x1"):
            for c in range(NCHLOC):
                p.dma("pool", out_d[0, c * 128:(c + 1) * 128, 0:S],
                      xbf[:, c * S:(c + 1) * S], reads=(xk[c],),
                      writes=("st0",), tag="dbgx")
        info = p.emit(final_waits=[("pool", "st0")])
        return nc, info
    info = p.emit(final_waits=[("pool", "st0"), ("pool", "st1")])
    return nc, info


def _ln_smalls_a(p, pcol, W, Dm, strow, tagp, s1c, s2c):
    ks1 = [f"ps{k}" for k in range(s1c // 512, (s1c + W + 511) // 512)]
    ks2 = [f"ps{k}" for k in range(s2c // 512, (s2c + W + 511) // 512)]
    p.op("act", lambda e: e.mul(out=strow[0:1, 0:W], in_=pcol(s1c, W, 0, 1),
                                mul=1.0 / Dm),
         reads=tuple(ks1), writes=("st_m",), tag=tagp + "mean")
    p.op("act", lambda e: e.mul(out=strow[0:1, 2 * W:3 * W],
                                in_=pcol(s2c, W, 0, 1), mul=1.0 / Dm),
         reads=tuple(ks2), writes=("st_v",), tag=tagp + "ey2")


def _ln_smalls_b(p, pcol, W, strow, rmB, onesrf, tagp, bccol,
                 roff=0, moff=None, rkey="rmBr", mkey="rmBm"):
    IWtot = rmB.shape[1] // 2 if moff is None else moff
    p.op("dve", lambda e: e.tensor_mul(strow[0:1, W:2 * W], strow[0:1, 0:W],
                                       strow[0:1, 0:W]),
         reads=("st_m",), writes=("st_t",), tag=tagp + "m2")
    p.op("dve", lambda e: e.tensor_sub(strow[0:1, W:2 * W],
                                       strow[0:1, 2 * W:3 * W],
                                       strow[0:1, W:2 * W]),
         reads=("st_t", "st_v"), writes=("st_t",), tag=tagp + "var")
    p.op("dve", lambda e: e.tensor_scalar_add(strow[0:1, W:2 * W],
                                              strow[0:1, W:2 * W], 1e-5),
         reads=("st_t",), writes=("st_t",), tag=tagp + "eps")
    p.op("act", lambda e: e.activation(out=strow[0:1, W:2 * W],
                                       in_=strow[0:1, W:2 * W], func=AF.Ln),
         reads=("st_t",), writes=("st_t",), tag=tagp + "ln")
    p.op("act", lambda e: e.activation(out=strow[0:1, W:2 * W],
                                       in_=strow[0:1, W:2 * W], func=AF.Exp,
                                       scale=-0.5),
         reads=("st_t",), writes=("st_t",), tag=tagp + "exp")
    p.op("dve", lambda e: e.tensor_mul(strow[0:1, 2 * W:3 * W],
                                       strow[0:1, 0:W], strow[0:1, W:2 * W]),
         reads=("st_m", "st_t"), writes=("st_v",), tag=tagp + "mr")
    rowb, onesb = onesrf
    nih = W // 512
    p.op("act", lambda e: e.copy(out=rowb[0:1, 0:W], in_=strow[0:1, W:2 * W]),
         reads=("st_t",), writes=("rowb_r",), tag=tagp + "rcv")
    p.op("act", lambda e: e.copy(out=rowb[0:1, W:2 * W],
                                 in_=strow[0:1, 2 * W:3 * W]),
         reads=("st_v",), writes=("rowb_m",), tag=tagp + "mcv")
    for ih in range(nih):
        p.op("pe", lambda e, ih=ih: e.matmul(
            pcol(bccol + ih * 512, 512), onesb,
            rowb[0:1, ih * 512: ih * 512 + 512],
            start=True, stop=True),
            reads=("onesr", "rowb_r"),
            writes=(f"ps{(bccol + ih * 512) // 512}",), tag=tagp + "rbc")
    p.op("dve", lambda e: e.tensor_copy(rmB[:, roff:roff + W],
                                        pcol(bccol, W)),
         reads=tuple(f"ps{k}" for k in range(bccol // 512,
                                             (bccol + W + 511) // 512)),
         writes=(rkey,), tag=tagp + "rbe")
    for ih in range(nih):
        p.op("pe", lambda e, ih=ih: e.matmul(
            pcol(bccol + ih * 512, 512), onesb,
            rowb[0:1, W + ih * 512: W + ih * 512 + 512],
            start=True, stop=True),
            reads=("onesr", "rowb_m"),
            writes=(f"ps{(bccol + ih * 512) // 512}",), tag=tagp + "mbc")
    p.op("dve", lambda e: e.tensor_copy(rmB[:, IWtot:IWtot + W],
                                        pcol(bccol, W)),
         reads=tuple(f"ps{k}" for k in range(bccol // 512,
                                             (bccol + W + 511) // 512)),
         writes=(mkey,), tag=tagp + "mbe")


def _ln_smalls(p, pcol, W, Dm, strow, rmB, onesrf, tagp, s1c, s2c, bccol,
               roff=0, moff=None, rkey="rmBr", mkey="rmBm"):
    _ln_smalls_a(p, pcol, W, Dm, strow, tagp, s1c, s2c)
    _ln_smalls_b(p, pcol, W, strow, rmB, onesrf, tagp, bccol,
                 roff=roff, moff=moff, rkey=rkey, mkey=mkey)


def _top2(p, gs, cq, gm, outkey):
    m1 = gm[:, 0:1]
    m2 = gm[:, 1:2]
    p.op("dve", lambda e: e.reduce_max(m1, gs, axis=AX),
         reads=("gt",), writes=("g_m1",))
    p.op("dve", lambda e: e.tensor_scalar(cq, gs, m1, -1e30,
                                          ALU.is_ge, ALU.mult),
         reads=("gt", "g_m1"), writes=(outkey,))
    p.op("dve", lambda e: e.tensor_add(cq, cq, gs),
         reads=(outkey, "gt"), writes=(outkey,))
    p.op("dve", lambda e: e.reduce_max(m2, cq, axis=AX),
         reads=(outkey,), writes=("g_m2",))
    p.op("dve", lambda e: e.tensor_scalar(cq, gs, m2, None,
                                          ALU.is_ge),
         reads=("gt", "g_m2"), writes=(outkey,))
    p.op("dve", lambda e: e.tensor_add(m1, m1, m2),
         reads=("g_m1", "g_m2"), writes=("g_m1",))
    p.op("dve", lambda e: e.tensor_scalar_add(m1, m1, 1e-6),
         reads=("g_m1",), writes=("g_m1",))
    p.op("dve", lambda e: e.reciprocal(m1, m1),
         reads=("g_m1",), writes=("g_m1",))
    p.op("dve", lambda e: e.scalar_tensor_tensor(cq, gs, m1, cq,
                                                 ALU.mult, ALU.mult),
         reads=("gt", "g_m1", outkey), writes=(outkey,))


_built = {}


def _get_nc(S, NB):
    key = (S, NB)
    if key not in _built:
        _built[key] = build_nc(S, NB)
    return _built[key]


def kernel(L0_attn_x, L0_attn_w_qkv, L0_attn_w_out, L0_ffn_gate_w,
           L0_ffn_balance_bias, L0_ffn_w1, L0_ffn_w2):
    x = np.asarray(L0_attn_x, dtype=np.float32)
    B, S, D_ = x.shape
    NB = B // 8
    nc, info = _get_nc(S, NB)

    x_fm = np.ascontiguousarray(x.transpose(0, 2, 1))
    wqkvT = np.ascontiguousarray(np.asarray(L0_attn_w_qkv, np.float32).T)
    woutT = np.ascontiguousarray(np.asarray(L0_attn_w_out, np.float32).T)
    gateT = np.ascontiguousarray(np.asarray(L0_ffn_gate_w, np.float32).T)
    bb = np.ascontiguousarray(
        np.asarray(L0_ffn_balance_bias, np.float32).reshape(1, E))
    w1T = np.ascontiguousarray(
        np.asarray(L0_ffn_w1, np.float32).transpose(0, 2, 1))
    w2T = np.ascontiguousarray(
        np.asarray(L0_ffn_w2, np.float32).transpose(0, 2, 1))

    in_maps = []
    for c in range(8):
        in_maps.append({
            "x": np.ascontiguousarray(x_fm[c * NB:(c + 1) * NB]),
            "wqkvT": wqkvT, "woutT": woutT, "gateT": gateT, "bb": bb,
            "w1T": w1T, "w2T": w2T,
        })
    res = run_bass_kernel_spmd(nc, in_maps, core_ids=list(range(8)))
    outs = [res.results[c]["out"] for c in range(8)]
    full = np.concatenate(outs, axis=0)
    return np.ascontiguousarray(full.transpose(0, 2, 1)).astype(np.float32)



# version 36
# speedup vs baseline: 1.0545x; 1.0545x over previous
import math
from contextlib import ExitStack

import numpy as np
import ml_dtypes

import concourse.bass as bass
import concourse.mybir as mybir
from concourse.bass_utils import run_bass_kernel_spmd

BF = ml_dtypes.bfloat16
F32 = mybir.dt.float32
BF16 = mybir.dt.bfloat16
AF = mybir.ActivationFunctionType
ALU = mybir.AluOpType
AX = mybir.AxisListType.X

D = 1024
H = 16
DH = 64
E = 8
EH = 2048
NCHLOC = D // 128


class Prog:
    STREAMS = ("pe", "act", "dve", "pool", "sync")

    NDMASEM = 8

    def __init__(self, nc):
        self.nc = nc
        self.ops = []
        self.last_w = {}
        self.readers = {}
        self.sem_last = {}
        self.dma_ord = {}

    EP = "EPOCH"

    def op(self, stream, fn, reads=(), writes=(), dma=False, tag="",
           lowp=False):
        opid = len(self.ops)
        deps = set()
        if self.EP not in writes:
            reads = tuple(reads) + (self.EP,)
        for b in reads:
            w = self.last_w.get(b)
            if w is not None:
                deps.add(w)
            self.readers.setdefault(b, []).append(opid)
        for b in writes:
            w = self.last_w.get(b)
            if w is not None:
                deps.add(w)
            war = {}
            for r in self.readers.get(b, ()):
                if r == opid:
                    continue
                ro = self.ops[r]
                war[self._sem_of(ro)] = max(war.get(self._sem_of(ro), -1), r)
            deps.update(war.values())
            self.readers[b] = []
            self.last_w[b] = opid
        deps.discard(opid)
        o = dict(id=opid, stream=stream, fn=fn, deps=sorted(deps),
                 dma=dma, tag=tag, lowp=lowp)
        if dma:
            k = self.dma_ord.get(stream, 0)
            self.dma_ord[stream] = k + 1
            o["sem"] = f"{stream}_d{k % self.NDMASEM}"
            o["dval"] = 16 * (k // self.NDMASEM + 1)
        self.ops.append(o)
        self.sem_last[self._sem_of(o)] = opid
        return opid

    def barrier(self, fn=None, stream="dve"):
        deps = set(self.sem_last.values())
        opid = len(self.ops)
        self.readers[self.EP] = []
        self.last_w[self.EP] = opid
        deps.discard(opid)
        o = dict(id=opid, stream=stream, fn=fn, deps=sorted(deps),
                 dma=False, tag="barrier")
        self.ops.append(o)
        self.sem_last[self._sem_of(o)] = opid
        return opid

    def dma(self, queue, out, in_, reads=(), writes=(), tag=""):
        assert queue in ("sync", "pool")
        return self.op(queue,
                       lambda e, o=out, i=in_: e.dma_start(out=o, in_=i),
                       reads=reads, writes=writes, dma=True, tag=tag)

    def _sem_of(self, o):
        if "sem" in o:
            return o["sem"]
        s = o["stream"]
        if s in ("pool", "sync"):
            return s + "_c"
        return s

    def emit(self, final_waits=()):
        nc = self.nc
        needed = set()
        for o in self.ops:
            for d in o["deps"]:
                dep = self.ops[d]
                if (dep["stream"] == "pe" and o["stream"] == "pe"
                        and not dep["dma"]):
                    continue
                if (dep["stream"] == o["stream"] and dep["dma"] and o["dma"]):
                    continue
                needed.add(d)
        fin = []
        for stream, buf in final_waits:
            w = self.last_w.get(buf)
            if w is not None:
                needed.add(w)
                fin.append((stream, w))
        cnt = {}
        incval = {}
        for o in self.ops:
            s = self._sem_of(o)
            if o["dma"]:
                cnt[s] = o["dval"]
                incval[o["id"]] = o["dval"]
            elif o["id"] in needed:
                cnt[s] = cnt.get(s, 0) + 1
                incval[o["id"]] = cnt[s]
        waits = {o["id"]: [] for o in self.ops}
        wm = {}
        for o in self.ops:
            ws = {}
            if o["dma"] and o["dval"] > 16:
                s = self._sem_of(o)
                v = o["dval"] - 16
                if v > wm.get((o["stream"], s), 0):
                    ws[s] = v
            for d in o["deps"]:
                dep = self.ops[d]
                if (dep["stream"] == "pe" and o["stream"] == "pe"
                        and not dep["dma"]):
                    continue
                if (dep["stream"] == o["stream"] and dep["dma"] and o["dma"]):
                    continue
                if d not in incval:
                    raise AssertionError(f"dep {d} of {o['id']} has no inc")
                s = self._sem_of(dep)
                v = incval[d]
                if v > wm.get((o["stream"], s), 0):
                    ws[s] = max(ws.get(s, 0), v)
            for s, v in ws.items():
                wm[(o["stream"], s)] = v
                waits[o["id"]].append((s, v))
        semnames = sorted(cnt.keys())
        sems = {s: nc.alloc_semaphore(f"sem_{s}") for s in semnames}
        stream_ops = {s: [o for o in self.ops if o["stream"] == s]
                      for s in self.STREAMS}
        fin_by_stream = {}
        for stream, w in fin:
            o = self.ops[w]
            fin_by_stream.setdefault(stream, []).append(
                (self._sem_of(o), incval[w]))

        def body(eng, stream):
            for o in stream_ops[stream]:
                for s, v in waits[o["id"]]:
                    eng.wait_ge(sems[s], v)
                if o.get("lowp"):
                    with nc.allow_low_precision("bf16 scale factor; tolerance "
                                                "calibrated for bf16 math"):
                        inst = o["fn"](eng)
                else:
                    inst = o["fn"](eng)
                iv = incval.get(o["id"])
                if iv is not None:
                    o_s = self._sem_of(o)
                    inst.then_inc(sems[o_s], 16 if o["dma"] else 1)
            for s, v in fin_by_stream.get(stream, ()):
                eng.wait_ge(sems[s], v)

        with nc.Block() as block:
            block.tensor(lambda e: body(e, "pe"))
            block.scalar(lambda e: body(e, "act"))
            block.vector(lambda e: body(e, "dve"))
            block.gpsimd(lambda e: body(e, "pool"))
            block.sync(lambda e: body(e, "sync"))
        return dict(n_ops=len(self.ops), n_incs=len(incval),
                    sem_finals={s: cnt[s] for s in semnames})


def build_nc(S=2048, NB=2, dbg=None):
    nc = bass.Bass("TRN2", target_bir_lowering=False)
    JT = S // 128
    IW = min(1024, S)
    NI2 = S // IW
    NIH = IW // 512
    NG = S // 512
    GP = [(g, g + 1) if g + 1 < NG else (g,) for g in range(0, NG, 2)]

    x_d = nc.dram_tensor("x", [NB, D, S], F32, kind="ExternalInput").ap()
    wqkvT_d = nc.dram_tensor("wqkvT", [D, 3 * D], F32, kind="ExternalInput").ap()
    woutT_d = nc.dram_tensor("woutT", [D, D], F32, kind="ExternalInput").ap()
    gateT_d = nc.dram_tensor("gateT", [D, E], F32, kind="ExternalInput").ap()
    bb_d = nc.dram_tensor("bb", [1, E], F32, kind="ExternalInput").ap()
    w1T_d = nc.dram_tensor("w1T", [E, D, EH], F32, kind="ExternalInput").ap()
    w2T_d = nc.dram_tensor("w2T", [E, EH, D], F32, kind="ExternalInput").ap()
    out_d = nc.dram_tensor("out", [NB, D, S], F32, kind="ExternalOutput").ap()
    wqkv_s = nc.dram_tensor("wqkv_s", [D, 3 * D], BF16, kind="Internal").ap()
    wout_s = nc.dram_tensor("wout_s", [D, D], BF16, kind="Internal").ap()
    w1_s = nc.dram_tensor("w1_s", [E, D, EH], BF16, kind="Internal").ap()
    w2_s = nc.dram_tensor("w2_s", [E, EH, D], BF16, kind="Internal").ap()
    identf = nc.inline_tensor(np.eye(128, dtype=np.float32), "identf").ap()
    onesr_b = nc.inline_tensor(np.ones((1, 128), dtype=BF), "onesr_b").ap()
    onesc_b = nc.inline_tensor(np.ones((128, 1), dtype=BF), "onesc_b").ap()
    onesr_f = nc.inline_tensor(np.ones((1, 128), dtype=np.float32), "onesr_f").ap()
    ones64_c = nc.inline_tensor(np.ones((128, 64), dtype=BF), "ones64_c").ap()

    off = [((nc.sbuf_base + 63) // 64) * 64]

    def at(name, shape, dtype, base=None):
        nonlocal off
        if base is None:
            base = off[0]
            b = ((base + 63) // 64) * 64
            t = nc.alloc_sbuf_tensor_at(name, shape, dtype, offset=b)
            sz = int(np.prod(shape[1:])) * (2 if dtype == BF16 else 4)
            off[0] = b + sz
            return t
        b = ((base + 63) // 64) * 64
        return nc.alloc_sbuf_tensor_at(name, shape, dtype, offset=b), b

    xbf = at("xbf", [128, NCHLOC * S], BF16)
    gate_b = at("gate_b", [128, NCHLOC * E], BF16)
    bbB = at("bbB", [128, E], F32)
    cbias = at("cbias", [128, NCHLOC], F32)
    id_f = at("id_f", [128, 128], F32)
    ones_r = at("ones_r", [1, 128], BF16)
    ones_c = at("ones_c", [128, 1], BF16)
    ones_rf = at("ones_rf", [1, 128], F32)
    ones64 = at("ones64", [128, 64], BF16)
    strow = at("strow", [1, 3 * IW], F32)
    rowb = at("rowb", [1, 2 * IW], BF16)
    cstat = at("cstat", [1, 64], F32)
    gm = at("gm", [128, 4], F32)
    ab = off[0]
    Kb, o1 = at("Kb", [128, NCHLOC * S], BF16, base=ab)
    Qb, o2 = at("Qb", [128, NCHLOC * S], BF16, base=o1 + NCHLOC * S * 2)
    Vp, o3 = at("Vp", [128, JT * H * 64], BF16, base=o2 + NCHLOC * S * 2)
    conc, o4 = at("conc", [128, NCHLOC * IW], BF16, base=o3 + JT * H * 64 * 2)
    wring, o5 = at("wring", [128, 2 * 4096], BF16, base=o4 + NCHLOC * IW * 2)
    ptmp, o6 = at("ptmp", [128, 3 * IW], BF16, base=o5 + 2 * 4096 * 2)
    dnb, o7 = at("dnb", [64, 2 * IW], BF16, base=o6 + 3 * IW * 2)
    rmB, o8 = at("rmB", [128, 2 * IW], BF16, base=o7 + 2 * IW * 2)
    sqt, o9 = at("sqt", [128, IW], BF16, base=o8 + 2 * IW * 2)
    concU, o10 = at("concU", [64, 2 * IW], BF16, base=o9 + IW * 2)
    bcbr, o11 = at("bcbr", [64, IW], BF16, base=o10 + 2 * IW * 2)
    at_end = o11 + IW * 2
    ws0, f1 = at("ws0", [128, 16384], BF16, base=ab)
    ws1, f2 = at("ws1", [128, 16384], BF16, base=f1 + 16384 * 2)
    ws2, f3 = at("ws2", [128, 16384], BF16, base=f2 + 16384 * 2)
    hp, f4 = at("hp", [128, 16 * 512], BF16, base=f3 + 16384 * 2)
    acc, f5 = at("acc", [128, NCHLOC * 1024], F32, base=f4 + 16 * 512 * 2)
    outf, f6 = at("outf", [128, 2 * 512], F32, base=f5 + NCHLOC * 1024 * 4)
    htmp, f7 = at("htmp", [128, 2 * 512], BF16, base=f6 + 2 * 512 * 4)
    combB, f8 = at("combB", [128, 6 * 512], BF16, base=f7 + 2 * 512 * 2)
    combq, f9 = at("combq", [128, NG * 4 * E], F32, base=f8 + 6 * 512 * 2)
    gtmp, f10 = at("gtmp", [128, 2 * E], F32, base=f9 + NG * 4 * E * 4)
    ff_end = f10 + 2 * E * 4
    hi = max(at_end, ff_end)
    assert hi <= nc.sbuf_top, f"SBUF over budget: {hi} > {nc.sbuf_top}"

    ps = nc.alloc_psum_tensor("ps", [128, 4096], F32)

    def pcol(c, w, p0=0, p1=128):
        assert c + w <= 4096
        return ps[p0:p1, c:c + w]

    def pbanks(c, w):
        return [f"ps{k}" for k in range(c // 512, (c + w + 511) // 512)]

    ones_rf_ap = ones_rf[:, :]
    nc._dbg = dict(xbf=xbf, Kb=Kb, Qb=Qb, Vp=Vp, conc=conc, acc=acc, hp=hp,
                   cbias=cbias, gm=gm, combq=combq, strow=strow, rmB=rmB)
    p = Prog(nc)
    EP = "EPOCH"

    def bump_epoch():
        p.barrier(lambda e: e.memset(cstat[0:1, 63:64], 0.0))

    p.dma("sync", id_f[:, :], identf[:, :], writes=("idf",))
    p.dma("sync", ones_r[:, :], onesr_b[:, :], writes=("onesr",))
    p.dma("sync", ones_c[:, :], onesc_b[:, :], writes=("onesc",))
    p.dma("sync", ones_rf[:, :], onesr_f[:, :], writes=("onesrf",))
    p.dma("sync", ones64[:, :], ones64_c[:, :], writes=("ones64",))
    p.dma("pool", wqkv_s[:, 0:128], wqkvT_d[:, 0:128], writes=("wqkv_sq0",))
    p.dma("pool", xbf[:, :].rearrange("p (c s) -> p c s", c=NCHLOC),
          x_d[0].rearrange("(c p) s -> p c s", p=128),
          reads=(EP,), writes=tuple(f"xbf{c}" for c in range(NCHLOC)),
          tag="xload0")
    p.dma("pool", wqkv_s[:, 128:D], wqkvT_d[:, 128:D], writes=("wqkv_sq",))
    p.dma("pool", wqkv_s[:, D:2 * D], wqkvT_d[:, D:2 * D], writes=("wqkv_sk",))
    p.dma("pool", wqkv_s[:, 2 * D:3 * D], wqkvT_d[:, 2 * D:3 * D],
          writes=("wqkv_sv",))
    p.dma("pool", wout_s[:, :], woutT_d[:, :], writes=("wout_s",))
    p.dma("pool", gate_b[:, :].rearrange("p (c e) -> p c e", c=NCHLOC),
          gateT_d.rearrange("(c p) e -> p c e", p=128), writes=("gateb",))
    p.dma("sync", cstat[0:1, 32:32 + E], bb_d[:, :], writes=("bbrow",))
    p.op("pe", lambda e: e.matmul(pcol(3584, E), ones_rf[:, :],
                                  cstat[0:1, 32:32 + E], start=True, stop=True),
         reads=("onesrf", "bbrow"), writes=("ps7",))
    p.op("act", lambda e: e.copy(out=bbB[:, :], in_=pcol(3584, E)),
         reads=("ps7",), writes=("bbB",))

    for b in range(NB):
        xk = tuple(f"xbf{c}" for c in range(NCHLOC))
        if b > 0:
            for c in range(NCHLOC):
                p.dma("pool", xbf[:, c * S:(c + 1) * S],
                      x_d[b, c * 128:(c + 1) * 128, :],
                      reads=(EP,), writes=(xk[c],), tag=f"xload{b}")

        bump_epoch()

        NQ = max(1, S // 512)
        QW = min(S, 512)
        NI2S = S // IW
        cslots = [[] for _ in range(2 * NCHLOC - 1 + 2 * JT + 1)]

        def cslot(i):
            return cslots[min(i, len(cslots) - 1)]

        csets = [("q", mc) for mc in range(NCHLOC)] + \
                [("k", mc) for mc in range(NCHLOC)]
        for si, (which, mc) in enumerate(csets):
            srcb = Qb if which == "q" else Kb
            skey = f"Qb{mc}" if which == "q" else f"Kb{mc}"
            hb = (si % 2) * S
            hk = tuple(f"cc{c}" for c in
                       range((si % 2) * NI2S, (si % 2 + 1) * NI2S))
            col = mc if which == "q" else NCHLOC + mc
            base_i = mc if which == "q" else NCHLOC + 1 + mc
            cslot(base_i).append(
                ("act", lambda e, srcb=srcb, mc=mc, hb=hb: e.activation(
                    out=conc[:, hb:hb + S], in_=srcb[:, mc * S:(mc + 1) * S],
                    func=AF.Square), (skey,), hk, "csq"))
            for q in range(NQ):
                cslot(base_i + 1).append(
                    ("pe", lambda e, q=q, hb=hb: e.matmul(
                        pcol(2048 + q * 512, QW, 0, 1), ones_c[:, :],
                        conc[:, hb + q * QW: hb + (q + 1) * QW],
                        start=True, stop=True), ("onesc",) + hk,
                     (f"ps{4 + q}",), "csum"))
            cslot(base_i + 1).append(
                ("dve", lambda e, col=col: e.reduce_max(
                    cstat[0:1, col:col + 1], pcol(2048, NQ * QW, 0, 1),
                    axis=AX),
                 tuple(f"ps{4 + q}" for q in range(NQ)),
                 (f"cm{col}",), "cred"))
        ct0 = 2 * NCHLOC + 2
        cslot(ct0).append(
            ("dve", lambda e: e.tensor_mul(
                cstat[0:1, 16:16 + NCHLOC], cstat[0:1, 0:NCHLOC],
                cstat[0:1, NCHLOC:2 * NCHLOC]),
             tuple(f"cm{c}" for c in range(2 * NCHLOC)), ("cprod",), "cmul"))
        cslot(ct0).append(
            ("act", lambda e: e.activation(
                out=cstat[0:1, 24:24 + NCHLOC],
                in_=cstat[0:1, 16:16 + NCHLOC], func=AF.Ln),
             ("cprod",), ("clnr",), "cln"))
        cslot(ct0 + 2).append(
            ("act", lambda e: e.activation(
                out=cstat[0:1, 40:40 + NCHLOC],
                in_=cstat[0:1, 24:24 + NCHLOC], func=AF.Exp, scale=0.5),
             ("clnr",), ("crow",), "cexp"))
        cslot(ct0 + 2).append(
            ("act", lambda e: e.mul(
                out=cstat[0:1, 48:48 + NCHLOC],
                in_=cstat[0:1, 40:40 + NCHLOC], mul=-0.125),
             ("crow",), ("crow2",), "cneg"))
        cslot(ct0 + 4).append(
            ("pe", lambda e: e.matmul(
                pcol(2560, NCHLOC), ones_rf[:, :],
                cstat[0:1, 48:48 + NCHLOC], start=True, stop=True),
             ("onesrf", "crow2"), ("ps5",), "cbc"))
        cslot(ct0 + 4).append(
            ("act", lambda e: e.copy(
                out=cbias[:, 0:NCHLOC], in_=pcol(2560, NCHLOC)),
             ("ps5",), tuple(f"cb{m}" for m in range(NCHLOC)), "cbcp"))

        def fire(i):
            for stream, fn, rds, wrs, tg in cslots[i]:
                p.op(stream, fn, reads=rds, writes=wrs, tag=tg)

        for blk in range(2 * NCHLOC):
            slot = blk % 2
            wsl = wring[:, slot * 4096: slot * 4096 + 1024]
            src = wqkv_s[:, blk * 128:(blk + 1) * 128]
            wkey = ("wqkv_sq0" if blk == 0 else
                    "wqkv_sq" if blk < NCHLOC else "wqkv_sk")
            p.dma("sync", wsl.rearrange("p (c n) -> p c n", c=NCHLOC),
                  src.rearrange("(c p) n -> p c n", p=128),
                  reads=(wkey, EP, f"wr{slot}"), writes=(f"wr{slot}",),
                  tag=f"qkvw{blk}")
            dst = Qb if blk < NCHLOC else Kb
            dk = "Qb" if blk < NCHLOC else "Kb"
            mc = blk % NCHLOC
            for tt in range(S // 512):
                pc = (tt % 4) * 512
                for fc in range(NCHLOC):
                    p.op("pe", lambda e, wsl=wsl, fc=fc, tt=tt, pc=pc:
                         e.matmul(pcol(pc, 512),
                                  wsl[:, fc * 128:(fc + 1) * 128],
                                  xbf[:, fc * S + tt * 512: fc * S + tt * 512 + 512],
                                  start=(fc == 0), stop=(fc == NCHLOC - 1)),
                         reads=(f"wr{slot}", xk[fc]),
                         writes=(f"ps{pc // 512}",), tag="qkvmm")
                p.op("dve", lambda e, dst=dst, mc=mc, tt=tt, pc=pc:
                     e.tensor_copy(dst[:, mc * S + tt * 512: mc * S + tt * 512 + 512],
                                   pcol(pc, 512)),
                     reads=(f"ps{pc // 512}",), writes=(f"{dk}{mc}",),
                     tag="qkvev")
            if blk >= 1:
                fire(blk - 1)
        for db in range(2):
            slot = db % 2
            wsl = wring[:, slot * 4096: slot * 4096 + 4096]
            src = wqkv_s[:, 2 * D + db * 512: 2 * D + (db + 1) * 512]
            p.dma("sync", wsl.rearrange("p (c n) -> p c n", c=NCHLOC),
                  src.rearrange("(c p) n -> p c n", p=128),
                  reads=("wqkv_sv", EP, f"wr{slot}"), writes=(f"wr{slot}",),
                  tag=f"vw{db}")
            for jt in range(JT):
                pc = (jt % 4) * 512
                for fc in range(NCHLOC):
                    p.op("pe", lambda e, wsl=wsl, fc=fc, jt=jt, pc=pc:
                         e.matmul(pcol(pc, 512),
                                  xbf[:, fc * S + jt * 128: fc * S + jt * 128 + 128],
                                  wsl[:, fc * 512:(fc + 1) * 512],
                                  start=(fc == 0), stop=(fc == NCHLOC - 1)),
                         reads=(f"wr{slot}", xk[fc]),
                         writes=(f"ps{pc // 512}",), tag="vmm")
                vv = Vp[:, jt * 1024 + db * 512: jt * 1024 + (db + 1) * 512]
                p.op("dve", lambda e, vv=vv, pc=pc:
                     e.tensor_copy(vv, pcol(pc, 512)),
                     reads=(f"ps{pc // 512}",), writes=(f"Vp{jt}",), tag="vev")
                fire(2 * NCHLOC - 1 + db * JT + jt)
        fire(len(cslots) - 1)

        if dbg == "qkv":
            for mc in range(NCHLOC):
                p.dma("pool", out_d[b, mc * 128:(mc + 1) * 128, 0:S],
                      Qb[:, mc * S:(mc + 1) * S], reads=(f"Qb{mc}",),
                      writes=("st0",), tag="dbgq")
            info = p.emit(final_waits=[("pool", "st0")])
            nc._dbg_handles = dict(Kb=Kb, Vp=Vp)
            return nc, info

        if b == 0:
            for e_ in range(E):
                p.dma("pool", w1_s[e_], w1T_d[e_],
                      reads=("wr0", "wr1"), writes=(f"w1s{e_}",))
                p.dma("pool", w2_s[e_], w2T_d[e_],
                      reads=("wr0", "wr1"), writes=(f"w2s{e_}",))

        def dn_evac(h):
            sl = (h % 2) * IW
            for ih in range(NIH):
                p.op("dve", lambda e, sl=sl, ih=ih: e.tensor_copy(
                    concU[0:64, sl + ih * 512: sl + ih * 512 + 512],
                    pcol(OB + ih * 512, 512, 0, 64)),
                     reads=(f"ps{(OB + ih * 512) // 512}",),
                     writes=(f"cu{h % 2}",), tag="cuev")
            for ih in range(NIH):
                p.op("dve", lambda e, sl=sl, ih=ih: e.tensor_copy(
                    dnb[0:64, sl + ih * 512: sl + ih * 512 + 512],
                    pcol(ih * 512, 512, 0, 64)),
                     reads=(f"ps{ih}",), writes=(f"dn{h % 2}",), tag="dncp")

        OB = 3072
        okeys = ("ps6", "ps7")

        def norm_head(h):
            mc = h // 2
            r0 = (h % 2) * 64
            sl = (h % 2) * IW
            p.op("dve", lambda e, sl=sl: e.reciprocal(
                bcbr[0:64, 0:IW], dnb[0:64, sl:sl + IW]),
                 reads=(f"dn{h % 2}",), writes=("bcbr",), tag="drcp",
                 lowp=True)
            p.op("dve", lambda e, mc=mc, r0=r0, sl=sl:
                 e.tensor_mul(conc[r0:r0 + 64, mc * IW:(mc + 1) * IW],
                              concU[0:64, sl:sl + IW], bcbr[0:64, 0:IW]),
                 reads=(f"cu{h % 2}", "bcbr"),
                 writes=(f"cc{mc}",),
                 tag="avnrm")

        pend_ln1 = [None]

        def ln1_tail(i2p):
            iop = i2p * IW
            _ln_smalls_b(p, pcol, IW, strow, rmB, (rowb, ones_r[:, :]), "at", bccol=1024)
            for ec in range(NCHLOC):
                p.op("dve", lambda e, ec=ec, iop=iop: e.tensor_mul(
                    sqt[:, 0:IW], xbf[:, ec * S + iop: ec * S + iop + IW],
                    rmB[:, 0:IW]),
                    reads=(xk[ec], "rmBr"), writes=("sqt",), tag="nrm1")
                p.op("dve", lambda e, ec=ec, iop=iop: e.tensor_sub(
                    xbf[:, ec * S + iop: ec * S + iop + IW], sqt[:, 0:IW],
                    rmB[:, IW:2 * IW]),
                    reads=("sqt", "rmBm", xk[ec]), writes=(xk[ec],),
                    tag="nrm2")

        for i2 in range(NI2):
            io = i2 * IW
            for h in range(H):
                mc = h // 2
                r0 = (h % 2) * 64

                def emit_s(j):
                    scol = 1024 + (j % 2) * 1024
                    for ih in range(NIH):
                        p.op("pe", lambda e, mc=mc, r0=r0, j=j, ih=ih, io=io,
                             scol=scol:
                             e.matmul(pcol(scol + ih * 512, 512),
                                      Kb[r0:r0 + 64,
                                         mc * S + j * 128: mc * S + j * 128 + 128],
                                      Qb[r0:r0 + 64,
                                         mc * S + io + ih * 512:
                                         mc * S + io + ih * 512 + 512],
                                      start=True, stop=True),
                             reads=(f"Kb{mc}", f"Qb{mc}"),
                             writes=(f"ps{(scol + ih * 512) // 512}",),
                             tag="smm")

                def emit_exp(j):
                    scol = 1024 + (j % 2) * 1024
                    pslot = j % 3
                    p.op("act", lambda e, pslot=pslot, mc=mc, scol=scol:
                         e.activation(out=ptmp[:, pslot * IW:(pslot + 1) * IW],
                                      in_=pcol(scol, IW), func=AF.Exp,
                                      bias=cbias[:, mc:mc + 1], scale=0.125),
                         reads=tuple(pbanks(scol, IW)) + (f"cb{mc}",),
                         writes=(f"pt{pslot}",), tag="exp")

                def emit_avden(j):
                    pslot = j % 3
                    for ih in range(NIH):
                        p.op("pe", lambda e, j=j, ih=ih, pslot=pslot, h=h:
                             e.matmul(pcol(OB + ih * 512, 512, 0, 64),
                                      Vp[:, (j * H + h) * 64:
                                         (j * H + h) * 64 + 64],
                                      ptmp[:, pslot * IW + ih * 512:
                                           pslot * IW + ih * 512 + 512],
                                      start=(j == 0), stop=(j == JT - 1)),
                             reads=(f"Vp{j}", f"pt{pslot}"),
                             writes=(f"ps{(OB + ih * 512) // 512}",),
                             tag="avmm")
                        p.op("pe", lambda e, j=j, ih=ih, pslot=pslot:
                             e.matmul(pcol(ih * 512, 512, 0, 64),
                                      ones64[:, :],
                                      ptmp[:, pslot * IW + ih * 512:
                                           pslot * IW + ih * 512 + 512],
                                      start=(j == 0), stop=(j == JT - 1)),
                             reads=("ones64", f"pt{pslot}"),
                             writes=(f"ps{ih}",),
                             tag="dnmm")

                emit_s(0)
                emit_s(1)
                emit_exp(0)
                if h > 0:
                    norm_head(h - 1)
                elif pend_ln1[0] is not None:
                    ln1_tail(pend_ln1[0])
                    pend_ln1[0] = None
                for j in range(2, JT):
                    emit_s(j)
                    emit_exp(j - 1)
                    emit_avden(j - 2)
                emit_exp(JT - 1)
                emit_avden(JT - 2)
                emit_avden(JT - 1)
                dn_evac(h)
            norm_head(H - 1)
            if dbg == "conc":
                for fc in range(NCHLOC):
                    p.dma("pool", out_d[b, fc * 128:(fc + 1) * 128, io:io + IW],
                          conc[:, fc * IW:(fc + 1) * IW], reads=(f"cc{fc}",),
                          writes=("st0",), tag="dbgc")
                continue
            def _pstats(ec):
                stgt = sqt if ec % 2 == 0 else ptmp
                skey = "sqt" if ec % 2 == 0 else "pt0"
                for ih in range(NIH):
                    p.op("pe", lambda e, ec=ec, io=io, ih=ih: e.matmul(
                        pcol(1024 + ih * 512, 512, 0, 1), ones_c[:, :],
                        xbf[:, ec * S + io + ih * 512:
                            ec * S + io + ih * 512 + 512],
                        start=(ec == 0), stop=(ec == NCHLOC - 1)),
                        reads=("onesc", xk[ec]),
                        writes=(f"ps{2 + ih}",), tag="s1mm")
                    p.op("pe", lambda e, ih=ih, ec=ec, stgt=stgt: e.matmul(
                        pcol(2048 + ih * 512, 512, 0, 1), ones_c[:, :],
                        stgt[:, ih * 512: ih * 512 + 512],
                        start=(ec == 0), stop=(ec == NCHLOC - 1)),
                        reads=("onesc", skey),
                        writes=(f"ps{4 + ih}",), tag="s2mm")

            for ec in range(NCHLOC):
                slot = ec % 2
                pb = 0 if ec % 2 == 0 else 3072
                wsl = wring[:, slot * 4096: slot * 4096 + 1024]
                src = wout_s[:, ec * 128:(ec + 1) * 128]
                p.dma("sync", wsl.rearrange("p (c n) -> p c n", c=NCHLOC),
                      src.rearrange("(c p) n -> p c n", p=128),
                      reads=("wout_s", EP, f"wr{slot}"), writes=(f"wr{slot}",),
                      tag="woutw")
                for ih in range(NIH):
                    for fc in range(NCHLOC):
                        p.op("pe", lambda e, wsl=wsl, fc=fc, ih=ih, pb=pb:
                             e.matmul(pcol(pb + ih * 512, 512),
                                      wsl[:, fc * 128:(fc + 1) * 128],
                                      conc[:, fc * IW + ih * 512:
                                           fc * IW + ih * 512 + 512],
                                      start=(fc == 0), stop=(fc == NCHLOC - 1)),
                             reads=(f"wr{slot}", f"cc{fc}"),
                             writes=(f"ps{(pb + ih * 512) // 512}",),
                             tag="pjmm")
                p.op("dve", lambda e, ec=ec, io=io, pb=pb: e.tensor_add(
                    xbf[:, ec * S + io: ec * S + io + IW],
                    xbf[:, ec * S + io: ec * S + io + IW], pcol(pb, IW)),
                    reads=tuple(pbanks(pb, IW)) + (xk[ec],), writes=(xk[ec],),
                    tag="resid1")
                stgt = sqt if ec % 2 == 0 else ptmp
                skey = "sqt" if ec % 2 == 0 else "pt0"
                p.op("act", lambda e, ec=ec, io=io, stgt=stgt: e.activation(
                    out=stgt[:, 0:IW],
                    in_=xbf[:, ec * S + io: ec * S + io + IW],
                    func=AF.Square), reads=(xk[ec],), writes=(skey,),
                    tag="lnsq")
                if ec >= 1:
                    _pstats(ec - 1)
            _pstats(NCHLOC - 1)
            _ln_smalls_a(p, pcol, IW, D, strow, "at", s1c=1024, s2c=2048)
            if dbg == "y1":
                pend_ln1[0] = None
            elif i2 == NI2 - 1:
                ln1_tail(i2)
            else:
                pend_ln1[0] = i2

        if dbg == "conc":
            break
        if dbg in ("y1", "x1"):
            break
        NBLK = NG * 4
        Gt = outf[:, 0:NBLK * E]
        M1 = outf[:, NBLK * E: NBLK * E + NBLK]
        M2 = outf[:, NBLK * E + NBLK: NBLK * E + 2 * NBLK]
        for t in range(NBLK):
            to = t * 128
            for fc in range(NCHLOC):
                p.op("pe", lambda e, fc=fc, to=to, t=t: e.matmul(
                    pcol(t * E, E), xbf[:, fc * S + to: fc * S + to + 128],
                    gate_b[:, fc * E:(fc + 1) * E],
                    start=(fc == 0), stop=(fc == NCHLOC - 1)),
                    reads=(xk[fc], "gateb"), writes=("ps0",),
                    tag="gmm")
        p.op("act", lambda e: e.copy(out=Gt, in_=pcol(0, NBLK * E)),
             reads=("ps0",), writes=("gt",), tag="gevac")
        for t in range(NBLK):
            p.op("dve", lambda e, t=t: e.tensor_add(
                Gt[:, t * E:(t + 1) * E], Gt[:, t * E:(t + 1) * E], bbB[:, :]),
                reads=("gt", "bbB"), writes=("gt",), tag="gadd")
        p.op("act", lambda e: e.activation(out=Gt, in_=Gt, func=AF.Sigmoid),
             reads=("gt",), writes=("gt",), tag="gsig")
        cqs = [combq[:, t * E:(t + 1) * E] for t in range(NBLK)]
        cqk = [f"cq{t // 4}{t % 4}" for t in range(NBLK)]
        for t in range(NBLK):
            p.op("dve", lambda e, t=t: e.reduce_max(
                M1[:, t:t + 1], Gt[:, t * E:(t + 1) * E], axis=AX),
                reads=("gt",), writes=("gm1",), tag="gm1")
        for t in range(NBLK):
            p.op("dve", lambda e, t=t: e.tensor_scalar(
                cqs[t], Gt[:, t * E:(t + 1) * E], M1[:, t:t + 1], -1e30,
                ALU.is_ge, ALU.mult),
                reads=("gt", "gm1"), writes=(cqk[t],), tag="gmask")
        for t in range(NBLK):
            p.op("dve", lambda e, t=t: e.tensor_add(
                cqs[t], cqs[t], Gt[:, t * E:(t + 1) * E]),
                reads=(cqk[t], "gt"), writes=(cqk[t],), tag="gmadd")
        for t in range(NBLK):
            p.op("dve", lambda e, t=t: e.reduce_max(
                M2[:, t:t + 1], cqs[t], axis=AX),
                reads=(cqk[t],), writes=("gm2",), tag="gm2")
        for t in range(NBLK):
            p.op("dve", lambda e, t=t: e.tensor_scalar(
                cqs[t], Gt[:, t * E:(t + 1) * E], M2[:, t:t + 1], None,
                ALU.is_ge),
                reads=("gt", "gm2"), writes=(cqk[t],), tag="gmask2")
        p.op("dve", lambda e: e.tensor_add(M1, M1, M2),
             reads=("gm1", "gm2"), writes=("gm1",))
        p.op("dve", lambda e: e.tensor_scalar_add(M1, M1, 1e-6),
             reads=("gm1",), writes=("gm1",))
        p.op("dve", lambda e: e.reciprocal(M1, M1),
             reads=("gm1",), writes=("gm1",))
        for t in range(NBLK):
            p.op("dve", lambda e, t=t: e.scalar_tensor_tensor(
                cqs[t], Gt[:, t * E:(t + 1) * E], M1[:, t:t + 1], cqs[t],
                ALU.mult, ALU.mult),
                reads=("gt", "gm1", cqk[t]), writes=(cqk[t],), tag="gcomb")

        prefetched_e0 = (S == 2048)
        if prefetched_e0:
            assert o1 - NCHLOC * S * 2 == f1 - 16384 * 2 and f1 == o1 \
                and f2 == o2, "ws/Kb/Qb alias layout changed"
            p.dma("sync", ws0[:, :].rearrange("p (c n) -> p c n", c=NCHLOC),
                  w1_s[0].rearrange("(c p) n -> p c n", p=128),
                  reads=("w1s0",),
                  writes=("ws0",) + tuple(f"Kb{m}" for m in range(NCHLOC)),
                  tag="w1pre")
            p.dma("sync", ws1[:, :].rearrange("p (c n) -> p c n", c=16),
                  w2_s[0].rearrange("(c p) n -> p c n", p=128),
                  reads=("w2s0",),
                  writes=("ws1",) + tuple(f"Qb{m}" for m in range(NCHLOC)),
                  tag="w2pre")
        bump_epoch()
        wslots = [ws0, ws1, ws2]
        wskeys = ["ws0", "ws1", "ws2"]
        si = 0

        def w_loads(e_, s1i, s2i):
            p.dma("sync",
                  wslots[s1i][:, :].rearrange("p (c n) -> p c n", c=NCHLOC),
                  w1_s[e_].rearrange("(c p) n -> p c n", p=128),
                  reads=(f"w1s{e_}", EP, wskeys[s1i]), writes=(wskeys[s1i],),
                  tag=f"w1l{e_}")
            p.dma("sync",
                  wslots[s2i][:, :].rearrange("p (c n) -> p c n", c=16),
                  w2_s[e_].rearrange("(c p) n -> p c n", p=128),
                  reads=(f"w2s{e_}", EP, wskeys[s2i]), writes=(wskeys[s2i],),
                  tag=f"w2l{e_}")

        for gpi, gp in enumerate(GP):
            for e_ in range(E):
                csl = si
                sl1 = wslots[si]
                k1 = wskeys[si]
                sl2 = wslots[(si + 1) % 3]
                k2 = wskeys[(si + 1) % 3]
                si = (si + 2) % 3
                if not (e_ == 0 and (prefetched_e0 or gpi > 0)):
                    w_loads(e_, csl, (csl + 1) % 3)
                for gi, g in enumerate(gp):
                    for tt in range(4):
                        p.op("pe", lambda e, e_=e_, gi=gi, tt=tt, g=g:
                             e.transpose(
                            pcol(1024 + gi * 512 + tt * 128, 128, 0, 1),
                            combq[:, (g * 4 + tt) * E + e_:
                                  (g * 4 + tt) * E + e_ + 1],
                            id_f[:, :]),
                            reads=(f"cq{g}{tt}", "idf"),
                            writes=(f"ps{2 + gi}",), tag="ctr")
                    p.op("act", lambda e, gi=gi: e.copy(
                        out=rowb[0:1, gi * 512:(gi + 1) * 512],
                        in_=pcol(1024 + gi * 512, 512, 0, 1)),
                        reads=(f"ps{2 + gi}",), writes=("rowb_r",),
                        tag="crev")
                    p.op("pe", lambda e, gi=gi: e.matmul(
                        pcol(1024 + gi * 512, 512), ones_r[:, :],
                        rowb[0:1, gi * 512:(gi + 1) * 512],
                        start=True, stop=True),
                        reads=("onesr", "rowb_r"), writes=(f"ps{2 + gi}",),
                        tag="cbcm")
                    p.op("dve", lambda e, gi=gi: e.tensor_copy(
                        combB[:, gi * 512:(gi + 1) * 512],
                        pcol(1024 + gi * 512, 512)),
                        reads=(f"ps{2 + gi}",), writes=(f"cB{gi}",),
                        tag="cbev")
                for gi, g in enumerate(gp):
                    go = g * 512
                    for ehc in range(16):
                        pc = (ehc % 3) * 512
                        for fc in range(NCHLOC):
                            p.op("pe", lambda e, sl1=sl1, fc=fc, ehc=ehc,
                                 go=go, pc=pc: e.matmul(
                                     pcol(pc, 512),
                                     sl1[:, fc * 2048 + ehc * 128:
                                         fc * 2048 + ehc * 128 + 128],
                                     xbf[:, fc * S + go: fc * S + go + 512],
                                     start=(fc == 0), stop=(fc == NCHLOC - 1)),
                                 reads=(k1, xk[fc]),
                                 writes=(f"ps{pc // 512}",), tag="amm")
                        hts = ehc % 2
                        p.op("act", lambda e, hts=hts, pc=pc: e.activation(
                            out=htmp[:, hts * 512:(hts + 1) * 512],
                            in_=pcol(pc, 512), func=AF.Silu),
                            reads=(f"ps{pc // 512}",), writes=(f"ht{hts}",),
                            tag="silu")
                        p.op("dve", lambda e, hts=hts, ehc=ehc, gi=gi:
                             e.tensor_mul(hp[:, ehc * 512:(ehc + 1) * 512],
                                          htmp[:, hts * 512:(hts + 1) * 512],
                                          combB[:, gi * 512:(gi + 1) * 512]),
                             reads=(f"ht{hts}", f"cB{gi}"),
                             writes=(f"hp{ehc}",), tag="hsc")
                    for dh in range(2):
                        base = dh * 2048
                        for ehc in range(16):
                            for dc in range(4):
                                d_ = dh * 4 + dc
                                p.op("pe", lambda e, sl2=sl2, ehc=ehc, d_=d_,
                                     base=base, dc=dc: e.matmul(
                                         pcol(base + dc * 512, 512),
                                         sl2[:, ehc * 1024 + d_ * 128:
                                             ehc * 1024 + d_ * 128 + 128],
                                         hp[:, ehc * 512:(ehc + 1) * 512],
                                         start=(ehc == 0), stop=(ehc == 15)),
                                     reads=(k2, f"hp{ehc}"),
                                     writes=(f"ps{(base + dc * 512) // 512}",),
                                     tag="bmm")
                        for dc in range(4):
                            d_ = dh * 4 + dc
                            ackey = f"ac{gi}{d_}"
                            asl = acc[:, (gi * NCHLOC + d_) * 512:
                                      (gi * NCHLOC + d_ + 1) * 512]
                            if e_ == 0:
                                p.op("dve", lambda e, asl=asl, base=base,
                                     dc=dc: e.tensor_copy(
                                         asl, pcol(base + dc * 512, 512)),
                                     reads=(f"ps{(base + dc * 512) // 512}",),
                                     writes=(ackey,), tag="acc0")
                            else:
                                p.op("dve", lambda e, asl=asl, base=base,
                                     dc=dc: e.tensor_add(
                                         asl, asl, pcol(base + dc * 512, 512)),
                                     reads=(f"ps{(base + dc * 512) // 512}",
                                            ackey),
                                     writes=(ackey,), tag="acc")
            if gpi + 1 < len(GP):
                w_loads(0, si, (si + 1) % 3)
            for gi, g in enumerate(gp):
                go = g * 512
                for ec in range(NCHLOC):
                    ackey = f"ac{gi}{ec}"
                    asl = acc[:, (gi * NCHLOC + ec) * 512:
                              (gi * NCHLOC + ec + 1) * 512]
                    hsl = hp[:, (gi * NCHLOC + ec) * 512:
                             (gi * NCHLOC + ec + 1) * 512]
                    hkey = f"hp{gi * NCHLOC + ec}"
                    p.op("dve", lambda e, asl=asl, hsl=hsl, ec=ec, go=go:
                         e.tensor_add(hsl,
                                      xbf[:, ec * S + go: ec * S + go + 512],
                                      asl),
                         reads=(ackey, xk[ec]), writes=(hkey,), tag="y2")
                    p.op("act", lambda e, hsl=hsl: e.activation(
                        out=htmp[:, 0:512], in_=hsl, func=AF.Square),
                        reads=(hkey,), writes=("ht0",), tag="l2sq")
                    p.op("pe", lambda e, ec=ec, gi=gi, hsl=hsl: e.matmul(
                        pcol(2048 + gi * 512, 512, 0, 1), ones_c[:, :], hsl,
                        start=(ec == 0), stop=(ec == NCHLOC - 1)),
                        reads=("onesc", hkey), writes=(f"ps{4 + gi}",),
                        tag="t1mm")
                    p.op("pe", lambda e, ec=ec, gi=gi: e.matmul(
                        pcol(3072 + gi * 512, 512, 0, 1), ones_c[:, :],
                        htmp[:, 0:512],
                        start=(ec == 0), stop=(ec == NCHLOC - 1)),
                        reads=("onesc", "ht0"), writes=(f"ps{6 + gi}",),
                        tag="t2mm")
            WW = 512 * len(gp)
            _ln_smalls(p, pcol, WW, D, strow, combB,
                       (rowb, ones_r[:, :]), "ff",
                       s1c=2048, s2c=3072, bccol=2048,
                       roff=1024, moff=2048, rkey="cB2", mkey="cB3")
            for gi, g in enumerate(gp):
                go = g * 512
                for ec in range(NCHLOC):
                    osl = ec % 2
                    hsl = hp[:, (gi * NCHLOC + ec) * 512:
                             (gi * NCHLOC + ec + 1) * 512]
                    hkey = f"hp{gi * NCHLOC + ec}"
                    p.op("dve", lambda e, hsl=hsl, gi=gi: e.tensor_mul(
                        htmp[:, 512:1024], hsl,
                        combB[:, 1024 + gi * 512: 1024 + (gi + 1) * 512]),
                        reads=(hkey, "cB2"), writes=("ht1",), tag="o1")
                    p.op("dve", lambda e, osl=osl, gi=gi: e.tensor_sub(
                        outf[:, osl * 512:(osl + 1) * 512], htmp[:, 512:1024],
                        combB[:, 2048 + gi * 512: 2048 + (gi + 1) * 512]),
                        reads=("ht1", "cB3"), writes=(f"of{osl}",), tag="o2")
                    p.dma("pool",
                          out_d[b, ec * 128:(ec + 1) * 128, go:go + 512],
                          outf[:, osl * 512:(osl + 1) * 512],
                          reads=(f"of{osl}",), writes=(f"st{osl}",),
                          tag="store")

    if dbg in ("conc", "y1", "x1"):
        if dbg in ("y1", "x1"):
            for c in range(NCHLOC):
                p.dma("pool", out_d[0, c * 128:(c + 1) * 128, 0:S],
                      xbf[:, c * S:(c + 1) * S], reads=(xk[c],),
                      writes=("st0",), tag="dbgx")
        info = p.emit(final_waits=[("pool", "st0")])
        return nc, info
    info = p.emit(final_waits=[("pool", "st0"), ("pool", "st1")])
    return nc, info


def _ln_smalls_a(p, pcol, W, Dm, strow, tagp, s1c, s2c):
    ks1 = [f"ps{k}" for k in range(s1c // 512, (s1c + W + 511) // 512)]
    ks2 = [f"ps{k}" for k in range(s2c // 512, (s2c + W + 511) // 512)]
    p.op("act", lambda e: e.mul(out=strow[0:1, 0:W], in_=pcol(s1c, W, 0, 1),
                                mul=1.0 / Dm),
         reads=tuple(ks1), writes=("st_m",), tag=tagp + "mean")
    p.op("act", lambda e: e.mul(out=strow[0:1, 2 * W:3 * W],
                                in_=pcol(s2c, W, 0, 1), mul=1.0 / Dm),
         reads=tuple(ks2), writes=("st_v",), tag=tagp + "ey2")


def _ln_smalls_b(p, pcol, W, strow, rmB, onesrf, tagp, bccol,
                 roff=0, moff=None, rkey="rmBr", mkey="rmBm"):
    IWtot = rmB.shape[1] // 2 if moff is None else moff
    p.op("dve", lambda e: e.tensor_mul(strow[0:1, W:2 * W], strow[0:1, 0:W],
                                       strow[0:1, 0:W]),
         reads=("st_m",), writes=("st_t",), tag=tagp + "m2")
    p.op("dve", lambda e: e.tensor_sub(strow[0:1, W:2 * W],
                                       strow[0:1, 2 * W:3 * W],
                                       strow[0:1, W:2 * W]),
         reads=("st_t", "st_v"), writes=("st_t",), tag=tagp + "var")
    p.op("dve", lambda e: e.tensor_scalar_add(strow[0:1, W:2 * W],
                                              strow[0:1, W:2 * W], 1e-5),
         reads=("st_t",), writes=("st_t",), tag=tagp + "eps")
    p.op("act", lambda e: e.activation(out=strow[0:1, W:2 * W],
                                       in_=strow[0:1, W:2 * W], func=AF.Ln),
         reads=("st_t",), writes=("st_t",), tag=tagp + "ln")
    p.op("act", lambda e: e.activation(out=strow[0:1, W:2 * W],
                                       in_=strow[0:1, W:2 * W], func=AF.Exp,
                                       scale=-0.5),
         reads=("st_t",), writes=("st_t",), tag=tagp + "exp")
    p.op("dve", lambda e: e.tensor_mul(strow[0:1, 2 * W:3 * W],
                                       strow[0:1, 0:W], strow[0:1, W:2 * W]),
         reads=("st_m", "st_t"), writes=("st_v",), tag=tagp + "mr")
    rowb, onesb = onesrf
    nih = W // 512
    p.op("act", lambda e: e.copy(out=rowb[0:1, 0:W], in_=strow[0:1, W:2 * W]),
         reads=("st_t",), writes=("rowb_r",), tag=tagp + "rcv")
    p.op("act", lambda e: e.copy(out=rowb[0:1, W:2 * W],
                                 in_=strow[0:1, 2 * W:3 * W]),
         reads=("st_v",), writes=("rowb_m",), tag=tagp + "mcv")
    for ih in range(nih):
        p.op("pe", lambda e, ih=ih: e.matmul(
            pcol(bccol + ih * 512, 512), onesb,
            rowb[0:1, ih * 512: ih * 512 + 512],
            start=True, stop=True),
            reads=("onesr", "rowb_r"),
            writes=(f"ps{(bccol + ih * 512) // 512}",), tag=tagp + "rbc")
    p.op("dve", lambda e: e.tensor_copy(rmB[:, roff:roff + W],
                                        pcol(bccol, W)),
         reads=tuple(f"ps{k}" for k in range(bccol // 512,
                                             (bccol + W + 511) // 512)),
         writes=(rkey,), tag=tagp + "rbe")
    for ih in range(nih):
        p.op("pe", lambda e, ih=ih: e.matmul(
            pcol(bccol + ih * 512, 512), onesb,
            rowb[0:1, W + ih * 512: W + ih * 512 + 512],
            start=True, stop=True),
            reads=("onesr", "rowb_m"),
            writes=(f"ps{(bccol + ih * 512) // 512}",), tag=tagp + "mbc")
    p.op("dve", lambda e: e.tensor_copy(rmB[:, IWtot:IWtot + W],
                                        pcol(bccol, W)),
         reads=tuple(f"ps{k}" for k in range(bccol // 512,
                                             (bccol + W + 511) // 512)),
         writes=(mkey,), tag=tagp + "mbe")


def _ln_smalls(p, pcol, W, Dm, strow, rmB, onesrf, tagp, s1c, s2c, bccol,
               roff=0, moff=None, rkey="rmBr", mkey="rmBm"):
    _ln_smalls_a(p, pcol, W, Dm, strow, tagp, s1c, s2c)
    _ln_smalls_b(p, pcol, W, strow, rmB, onesrf, tagp, bccol,
                 roff=roff, moff=moff, rkey=rkey, mkey=mkey)


def _top2(p, gs, cq, gm, outkey):
    m1 = gm[:, 0:1]
    m2 = gm[:, 1:2]
    p.op("dve", lambda e: e.reduce_max(m1, gs, axis=AX),
         reads=("gt",), writes=("g_m1",))
    p.op("dve", lambda e: e.tensor_scalar(cq, gs, m1, -1e30,
                                          ALU.is_ge, ALU.mult),
         reads=("gt", "g_m1"), writes=(outkey,))
    p.op("dve", lambda e: e.tensor_add(cq, cq, gs),
         reads=(outkey, "gt"), writes=(outkey,))
    p.op("dve", lambda e: e.reduce_max(m2, cq, axis=AX),
         reads=(outkey,), writes=("g_m2",))
    p.op("dve", lambda e: e.tensor_scalar(cq, gs, m2, None,
                                          ALU.is_ge),
         reads=("gt", "g_m2"), writes=(outkey,))
    p.op("dve", lambda e: e.tensor_add(m1, m1, m2),
         reads=("g_m1", "g_m2"), writes=("g_m1",))
    p.op("dve", lambda e: e.tensor_scalar_add(m1, m1, 1e-6),
         reads=("g_m1",), writes=("g_m1",))
    p.op("dve", lambda e: e.reciprocal(m1, m1),
         reads=("g_m1",), writes=("g_m1",))
    p.op("dve", lambda e: e.scalar_tensor_tensor(cq, gs, m1, cq,
                                                 ALU.mult, ALU.mult),
         reads=("gt", "g_m1", outkey), writes=(outkey,))


_built = {}


def _get_nc(S, NB):
    key = (S, NB)
    if key not in _built:
        _built[key] = build_nc(S, NB)
    return _built[key]


def kernel(L0_attn_x, L0_attn_w_qkv, L0_attn_w_out, L0_ffn_gate_w,
           L0_ffn_balance_bias, L0_ffn_w1, L0_ffn_w2):
    x = np.asarray(L0_attn_x, dtype=np.float32)
    B, S, D_ = x.shape
    NB = B // 8
    nc, info = _get_nc(S, NB)

    x_fm = np.ascontiguousarray(x.transpose(0, 2, 1))
    wqkvT = np.ascontiguousarray(np.asarray(L0_attn_w_qkv, np.float32).T)
    woutT = np.ascontiguousarray(np.asarray(L0_attn_w_out, np.float32).T)
    gateT = np.ascontiguousarray(np.asarray(L0_ffn_gate_w, np.float32).T)
    bb = np.ascontiguousarray(
        np.asarray(L0_ffn_balance_bias, np.float32).reshape(1, E))
    w1T = np.ascontiguousarray(
        np.asarray(L0_ffn_w1, np.float32).transpose(0, 2, 1))
    w2T = np.ascontiguousarray(
        np.asarray(L0_ffn_w2, np.float32).transpose(0, 2, 1))

    in_maps = []
    for c in range(8):
        in_maps.append({
            "x": np.ascontiguousarray(x_fm[c * NB:(c + 1) * NB]),
            "wqkvT": wqkvT, "woutT": woutT, "gateT": gateT, "bb": bb,
            "w1T": w1T, "w2T": w2T,
        })
    res = run_bass_kernel_spmd(nc, in_maps, core_ids=list(range(8)))
    outs = [res.results[c]["out"] for c in range(8)]
    full = np.concatenate(outs, axis=0)
    return np.ascontiguousarray(full.transpose(0, 2, 1)).astype(np.float32)



# version 37
# speedup vs baseline: 1.0606x; 1.0058x over previous
import math
from contextlib import ExitStack

import numpy as np
import ml_dtypes

import concourse.bass as bass
import concourse.mybir as mybir
from concourse.bass_utils import run_bass_kernel_spmd

BF = ml_dtypes.bfloat16
F32 = mybir.dt.float32
BF16 = mybir.dt.bfloat16
AF = mybir.ActivationFunctionType
ALU = mybir.AluOpType
AX = mybir.AxisListType.X

D = 1024
H = 16
DH = 64
E = 8
EH = 2048
NCHLOC = D // 128


class Prog:
    STREAMS = ("pe", "act", "dve", "pool", "sync")

    NDMASEM = 8

    def __init__(self, nc):
        self.nc = nc
        self.ops = []
        self.last_w = {}
        self.readers = {}
        self.sem_last = {}
        self.dma_ord = {}

    EP = "EPOCH"

    def op(self, stream, fn, reads=(), writes=(), dma=False, tag="",
           lowp=False):
        opid = len(self.ops)
        deps = set()
        if self.EP not in writes:
            reads = tuple(reads) + (self.EP,)
        for b in reads:
            w = self.last_w.get(b)
            if w is not None:
                deps.add(w)
            self.readers.setdefault(b, []).append(opid)
        for b in writes:
            w = self.last_w.get(b)
            if w is not None:
                deps.add(w)
            war = {}
            for r in self.readers.get(b, ()):
                if r == opid:
                    continue
                ro = self.ops[r]
                war[self._sem_of(ro)] = max(war.get(self._sem_of(ro), -1), r)
            deps.update(war.values())
            self.readers[b] = []
            self.last_w[b] = opid
        deps.discard(opid)
        o = dict(id=opid, stream=stream, fn=fn, deps=sorted(deps),
                 dma=dma, tag=tag, lowp=lowp)
        if dma:
            k = self.dma_ord.get(stream, 0)
            self.dma_ord[stream] = k + 1
            o["sem"] = f"{stream}_d{k % self.NDMASEM}"
            o["dval"] = 16 * (k // self.NDMASEM + 1)
        self.ops.append(o)
        self.sem_last[self._sem_of(o)] = opid
        return opid

    def barrier(self, fn=None, stream="dve"):
        deps = set(self.sem_last.values())
        opid = len(self.ops)
        self.readers[self.EP] = []
        self.last_w[self.EP] = opid
        deps.discard(opid)
        o = dict(id=opid, stream=stream, fn=fn, deps=sorted(deps),
                 dma=False, tag="barrier")
        self.ops.append(o)
        self.sem_last[self._sem_of(o)] = opid
        return opid

    def dma(self, queue, out, in_, reads=(), writes=(), tag=""):
        assert queue in ("sync", "pool")
        return self.op(queue,
                       lambda e, o=out, i=in_: e.dma_start(out=o, in_=i),
                       reads=reads, writes=writes, dma=True, tag=tag)

    def _sem_of(self, o):
        if "sem" in o:
            return o["sem"]
        s = o["stream"]
        if s in ("pool", "sync"):
            return s + "_c"
        return s

    def emit(self, final_waits=()):
        nc = self.nc
        needed = set()
        for o in self.ops:
            for d in o["deps"]:
                dep = self.ops[d]
                if (dep["stream"] == "pe" and o["stream"] == "pe"
                        and not dep["dma"]):
                    continue
                if (dep["stream"] == o["stream"] and dep["dma"] and o["dma"]):
                    continue
                needed.add(d)
        fin = []
        for stream, buf in final_waits:
            w = self.last_w.get(buf)
            if w is not None:
                needed.add(w)
                fin.append((stream, w))
        cnt = {}
        incval = {}
        for o in self.ops:
            s = self._sem_of(o)
            if o["dma"]:
                cnt[s] = o["dval"]
                incval[o["id"]] = o["dval"]
            elif o["id"] in needed:
                cnt[s] = cnt.get(s, 0) + 1
                incval[o["id"]] = cnt[s]
        waits = {o["id"]: [] for o in self.ops}
        wm = {}
        for o in self.ops:
            ws = {}
            if o["dma"] and o["dval"] > 16:
                s = self._sem_of(o)
                v = o["dval"] - 16
                if v > wm.get((o["stream"], s), 0):
                    ws[s] = v
            for d in o["deps"]:
                dep = self.ops[d]
                if (dep["stream"] == "pe" and o["stream"] == "pe"
                        and not dep["dma"]):
                    continue
                if (dep["stream"] == o["stream"] and dep["dma"] and o["dma"]):
                    continue
                if d not in incval:
                    raise AssertionError(f"dep {d} of {o['id']} has no inc")
                s = self._sem_of(dep)
                v = incval[d]
                if v > wm.get((o["stream"], s), 0):
                    ws[s] = max(ws.get(s, 0), v)
            for s, v in ws.items():
                wm[(o["stream"], s)] = v
                waits[o["id"]].append((s, v))
        semnames = sorted(cnt.keys())
        sems = {s: nc.alloc_semaphore(f"sem_{s}") for s in semnames}
        stream_ops = {s: [o for o in self.ops if o["stream"] == s]
                      for s in self.STREAMS}
        fin_by_stream = {}
        for stream, w in fin:
            o = self.ops[w]
            fin_by_stream.setdefault(stream, []).append(
                (self._sem_of(o), incval[w]))

        def body(eng, stream):
            for o in stream_ops[stream]:
                for s, v in waits[o["id"]]:
                    eng.wait_ge(sems[s], v)
                if o.get("lowp"):
                    with nc.allow_low_precision("bf16 scale factor; tolerance "
                                                "calibrated for bf16 math"):
                        inst = o["fn"](eng)
                else:
                    inst = o["fn"](eng)
                iv = incval.get(o["id"])
                if iv is not None:
                    o_s = self._sem_of(o)
                    inst.then_inc(sems[o_s], 16 if o["dma"] else 1)
            for s, v in fin_by_stream.get(stream, ()):
                eng.wait_ge(sems[s], v)

        with nc.Block() as block:
            block.tensor(lambda e: body(e, "pe"))
            block.scalar(lambda e: body(e, "act"))
            block.vector(lambda e: body(e, "dve"))
            block.gpsimd(lambda e: body(e, "pool"))
            block.sync(lambda e: body(e, "sync"))
        return dict(n_ops=len(self.ops), n_incs=len(incval),
                    sem_finals={s: cnt[s] for s in semnames})


def build_nc(S=2048, NB=2, dbg=None):
    nc = bass.Bass("TRN2", target_bir_lowering=False)
    JT = S // 128
    IW = min(1024, S)
    NI2 = S // IW
    NIH = IW // 512
    NG = S // 512
    GP = [(g, g + 1) if g + 1 < NG else (g,) for g in range(0, NG, 2)]

    x_d = nc.dram_tensor("x", [NB, D, S], F32, kind="ExternalInput").ap()
    wqkvT_d = nc.dram_tensor("wqkvT", [D, 3 * D], F32, kind="ExternalInput").ap()
    woutT_d = nc.dram_tensor("woutT", [D, D], F32, kind="ExternalInput").ap()
    gateT_d = nc.dram_tensor("gateT", [D, E], F32, kind="ExternalInput").ap()
    bb_d = nc.dram_tensor("bb", [1, E], F32, kind="ExternalInput").ap()
    w1T_d = nc.dram_tensor("w1T", [E, D, EH], F32, kind="ExternalInput").ap()
    w2T_d = nc.dram_tensor("w2T", [E, EH, D], F32, kind="ExternalInput").ap()
    out_d = nc.dram_tensor("out", [NB, D, S], F32, kind="ExternalOutput").ap()
    wqkv_s = nc.dram_tensor("wqkv_s", [D, 3 * D], BF16, kind="Internal").ap()
    wout_s = nc.dram_tensor("wout_s", [D, D], BF16, kind="Internal").ap()
    w1_s = nc.dram_tensor("w1_s", [E, D, EH], BF16, kind="Internal").ap()
    w2_s = nc.dram_tensor("w2_s", [E, EH, D], BF16, kind="Internal").ap()
    identf = nc.inline_tensor(np.eye(128, dtype=np.float32), "identf").ap()
    onesr_b = nc.inline_tensor(np.ones((1, 128), dtype=BF), "onesr_b").ap()
    onesc_b = nc.inline_tensor(np.ones((128, 1), dtype=BF), "onesc_b").ap()
    onesr_f = nc.inline_tensor(np.ones((1, 128), dtype=np.float32), "onesr_f").ap()
    ones64_c = nc.inline_tensor(np.ones((128, 64), dtype=BF), "ones64_c").ap()

    off = [((nc.sbuf_base + 63) // 64) * 64]

    def at(name, shape, dtype, base=None):
        nonlocal off
        if base is None:
            base = off[0]
            b = ((base + 63) // 64) * 64
            t = nc.alloc_sbuf_tensor_at(name, shape, dtype, offset=b)
            sz = int(np.prod(shape[1:])) * (2 if dtype == BF16 else 4)
            off[0] = b + sz
            return t
        b = ((base + 63) // 64) * 64
        return nc.alloc_sbuf_tensor_at(name, shape, dtype, offset=b), b

    xbf = at("xbf", [128, NCHLOC * S], BF16)
    gate_b = at("gate_b", [128, NCHLOC * E], BF16)
    bbB = at("bbB", [128, E], F32)
    cbias = at("cbias", [128, NCHLOC], F32)
    id_f = at("id_f", [128, 128], F32)
    ones_r = at("ones_r", [1, 128], BF16)
    ones_c = at("ones_c", [128, 1], BF16)
    ones_rf = at("ones_rf", [1, 128], F32)
    ones64 = at("ones64", [128, 64], BF16)
    strow = at("strow", [1, 3 * IW], F32)
    rowb = at("rowb", [1, 2 * IW], BF16)
    cstat = at("cstat", [1, 64], F32)
    gm = at("gm", [128, 4], F32)
    ab = off[0]
    Kb, o1 = at("Kb", [128, NCHLOC * S], BF16, base=ab)
    Qb, o2 = at("Qb", [128, NCHLOC * S], BF16, base=o1 + NCHLOC * S * 2)
    Vp, o3 = at("Vp", [128, JT * H * 64], BF16, base=o2 + NCHLOC * S * 2)
    conc, o4 = at("conc", [128, NCHLOC * IW], BF16, base=o3 + JT * H * 64 * 2)
    wring, o5 = at("wring", [128, 2 * 4096], BF16, base=o4 + NCHLOC * IW * 2)
    ptmp, o6 = at("ptmp", [128, 3 * IW], BF16, base=o5 + 2 * 4096 * 2)
    dnb, o7 = at("dnb", [64, 2 * IW], BF16, base=o6 + 3 * IW * 2)
    rmB, o8 = at("rmB", [128, 2 * IW], BF16, base=o7 + 2 * IW * 2)
    sqt, o9 = at("sqt", [128, IW], BF16, base=o8 + 2 * IW * 2)
    concU, o10 = at("concU", [64, 2 * IW], BF16, base=o9 + IW * 2)
    bcbr, o11 = at("bcbr", [64, IW], BF16, base=o10 + 2 * IW * 2)
    at_end = o11 + IW * 2
    ws0, f1 = at("ws0", [128, 16384], BF16, base=ab)
    ws1, f2 = at("ws1", [128, 16384], BF16, base=f1 + 16384 * 2)
    ws2, f3 = at("ws2", [128, 16384], BF16, base=f2 + 16384 * 2)
    hp, f4 = at("hp", [128, 16 * 512], BF16, base=f3 + 16384 * 2)
    acc, f5 = at("acc", [128, NCHLOC * 1024], F32, base=f4 + 16 * 512 * 2)
    outf, f6 = at("outf", [128, 2 * 512], F32, base=f5 + NCHLOC * 1024 * 4)
    htmp, f7 = at("htmp", [128, 2 * 512], BF16, base=f6 + 2 * 512 * 4)
    combB, f8 = at("combB", [128, 6 * 512], BF16, base=f7 + 2 * 512 * 2)
    combq, f9 = at("combq", [128, NG * 4 * E], F32, base=f8 + 6 * 512 * 2)
    gtmp, f10 = at("gtmp", [128, 2 * E], F32, base=f9 + NG * 4 * E * 4)
    ff_end = f10 + 2 * E * 4
    hi = max(at_end, ff_end)
    assert hi <= nc.sbuf_top, f"SBUF over budget: {hi} > {nc.sbuf_top}"

    ps = nc.alloc_psum_tensor("ps", [128, 4096], F32)

    def pcol(c, w, p0=0, p1=128):
        assert c + w <= 4096
        return ps[p0:p1, c:c + w]

    def pbanks(c, w):
        return [f"ps{k}" for k in range(c // 512, (c + w + 511) // 512)]

    ones_rf_ap = ones_rf[:, :]
    nc._dbg = dict(xbf=xbf, Kb=Kb, Qb=Qb, Vp=Vp, conc=conc, acc=acc, hp=hp,
                   cbias=cbias, gm=gm, combq=combq, strow=strow, rmB=rmB)
    p = Prog(nc)
    EP = "EPOCH"

    def bump_epoch():
        p.barrier(lambda e: e.memset(cstat[0:1, 63:64], 0.0))

    p.dma("sync", id_f[:, :], identf[:, :], writes=("idf",))
    p.dma("sync", ones_r[:, :], onesr_b[:, :], writes=("onesr",))
    p.dma("sync", ones_c[:, :], onesc_b[:, :], writes=("onesc",))
    p.dma("sync", ones_rf[:, :], onesr_f[:, :], writes=("onesrf",))
    p.dma("sync", ones64[:, :], ones64_c[:, :], writes=("ones64",))
    p.dma("pool", wqkv_s[:, 0:128], wqkvT_d[:, 0:128], writes=("wqkv_sq0",))
    p.dma("pool", xbf[:, :].rearrange("p (c s) -> p c s", c=NCHLOC),
          x_d[0].rearrange("(c p) s -> p c s", p=128),
          reads=(EP,), writes=tuple(f"xbf{c}" for c in range(NCHLOC)),
          tag="xload0")
    p.dma("pool", wqkv_s[:, 128:D], wqkvT_d[:, 128:D], writes=("wqkv_sq",))
    p.dma("pool", wqkv_s[:, D:2 * D], wqkvT_d[:, D:2 * D], writes=("wqkv_sk",))
    p.dma("pool", wqkv_s[:, 2 * D:3 * D], wqkvT_d[:, 2 * D:3 * D],
          writes=("wqkv_sv",))
    p.dma("pool", wout_s[:, :], woutT_d[:, :], writes=("wout_s",))
    p.dma("pool", gate_b[:, :].rearrange("p (c e) -> p c e", c=NCHLOC),
          gateT_d.rearrange("(c p) e -> p c e", p=128), writes=("gateb",))
    p.dma("sync", cstat[0:1, 32:32 + E], bb_d[:, :], writes=("bbrow",))
    p.op("pe", lambda e: e.matmul(pcol(3584, E), ones_rf[:, :],
                                  cstat[0:1, 32:32 + E], start=True, stop=True),
         reads=("onesrf", "bbrow"), writes=("ps7",))
    p.op("act", lambda e: e.copy(out=bbB[:, :], in_=pcol(3584, E)),
         reads=("ps7",), writes=("bbB",))

    for b in range(NB):
        xk = tuple(f"xbf{c}" for c in range(NCHLOC))
        if b > 0:
            for c in range(NCHLOC):
                p.dma("pool", xbf[:, c * S:(c + 1) * S],
                      x_d[b, c * 128:(c + 1) * 128, :],
                      reads=(EP,), writes=(xk[c],), tag=f"xload{b}")

        bump_epoch()

        NQ = max(1, S // 512)
        QW = min(S, 512)
        NI2S = S // IW
        cslots = [[] for _ in range(2 * NCHLOC - 1 + 2 * JT + 1)]

        def cslot(i):
            return cslots[min(i, len(cslots) - 1)]

        csets = [("q", mc) for mc in range(NCHLOC)] + \
                [("k", mc) for mc in range(NCHLOC)]
        for si, (which, mc) in enumerate(csets):
            srcb = Qb if which == "q" else Kb
            skey = f"Qb{mc}" if which == "q" else f"Kb{mc}"
            hb = (si % 2) * S
            hk = tuple(f"cc{c}" for c in
                       range((si % 2) * NI2S, (si % 2 + 1) * NI2S))
            col = mc if which == "q" else NCHLOC + mc
            base_i = mc if which == "q" else NCHLOC + 1 + mc
            cslot(base_i).append(
                ("act", lambda e, srcb=srcb, mc=mc, hb=hb: e.activation(
                    out=conc[:, hb:hb + S], in_=srcb[:, mc * S:(mc + 1) * S],
                    func=AF.Square), (skey,), hk, "csq"))
            for q in range(NQ):
                cslot(base_i + 1).append(
                    ("pe", lambda e, q=q, hb=hb: e.matmul(
                        pcol(2048 + q * 512, QW, 0, 1), ones_c[:, :],
                        conc[:, hb + q * QW: hb + (q + 1) * QW],
                        start=True, stop=True), ("onesc",) + hk,
                     (f"ps{4 + q}",), "csum"))
            cslot(base_i + 1).append(
                ("dve", lambda e, col=col: e.reduce_max(
                    cstat[0:1, col:col + 1], pcol(2048, NQ * QW, 0, 1),
                    axis=AX),
                 tuple(f"ps{4 + q}" for q in range(NQ)),
                 (f"cm{col}",), "cred"))
        ct0 = 2 * NCHLOC + 2
        cslot(ct0).append(
            ("dve", lambda e: e.tensor_mul(
                cstat[0:1, 16:16 + NCHLOC], cstat[0:1, 0:NCHLOC],
                cstat[0:1, NCHLOC:2 * NCHLOC]),
             tuple(f"cm{c}" for c in range(2 * NCHLOC)), ("cprod",), "cmul"))
        cslot(ct0).append(
            ("act", lambda e: e.activation(
                out=cstat[0:1, 24:24 + NCHLOC],
                in_=cstat[0:1, 16:16 + NCHLOC], func=AF.Ln),
             ("cprod",), ("clnr",), "cln"))
        cslot(ct0 + 2).append(
            ("act", lambda e: e.activation(
                out=cstat[0:1, 40:40 + NCHLOC],
                in_=cstat[0:1, 24:24 + NCHLOC], func=AF.Exp, scale=0.5),
             ("clnr",), ("crow",), "cexp"))
        cslot(ct0 + 2).append(
            ("act", lambda e: e.mul(
                out=cstat[0:1, 48:48 + NCHLOC],
                in_=cstat[0:1, 40:40 + NCHLOC], mul=-0.125),
             ("crow",), ("crow2",), "cneg"))
        cslot(ct0 + 4).append(
            ("pe", lambda e: e.matmul(
                pcol(2560, NCHLOC), ones_rf[:, :],
                cstat[0:1, 48:48 + NCHLOC], start=True, stop=True),
             ("onesrf", "crow2"), ("ps5",), "cbc"))
        cslot(ct0 + 4).append(
            ("act", lambda e: e.copy(
                out=cbias[:, 0:NCHLOC], in_=pcol(2560, NCHLOC)),
             ("ps5",), tuple(f"cb{m}" for m in range(NCHLOC)), "cbcp"))

        def fire(i):
            for stream, fn, rds, wrs, tg in cslots[i]:
                p.op(stream, fn, reads=rds, writes=wrs, tag=tg)

        for blk in range(2 * NCHLOC):
            slot = blk % 2
            wsl = wring[:, slot * 4096: slot * 4096 + 1024]
            src = wqkv_s[:, blk * 128:(blk + 1) * 128]
            wkey = ("wqkv_sq0" if blk == 0 else
                    "wqkv_sq" if blk < NCHLOC else "wqkv_sk")
            p.dma("sync", wsl.rearrange("p (c n) -> p c n", c=NCHLOC),
                  src.rearrange("(c p) n -> p c n", p=128),
                  reads=(wkey, EP, f"wr{slot}"), writes=(f"wr{slot}",),
                  tag=f"qkvw{blk}")
            dst = Qb if blk < NCHLOC else Kb
            dk = "Qb" if blk < NCHLOC else "Kb"
            mc = blk % NCHLOC
            for tt in range(S // 512):
                pc = (tt % 4) * 512
                for fc in range(NCHLOC):
                    p.op("pe", lambda e, wsl=wsl, fc=fc, tt=tt, pc=pc:
                         e.matmul(pcol(pc, 512),
                                  wsl[:, fc * 128:(fc + 1) * 128],
                                  xbf[:, fc * S + tt * 512: fc * S + tt * 512 + 512],
                                  start=(fc == 0), stop=(fc == NCHLOC - 1)),
                         reads=(f"wr{slot}", xk[fc]),
                         writes=(f"ps{pc // 512}",), tag="qkvmm")
                p.op("dve", lambda e, dst=dst, mc=mc, tt=tt, pc=pc:
                     e.tensor_copy(dst[:, mc * S + tt * 512: mc * S + tt * 512 + 512],
                                   pcol(pc, 512)),
                     reads=(f"ps{pc // 512}",), writes=(f"{dk}{mc}",),
                     tag="qkvev")
            if blk >= 1:
                fire(blk - 1)
        for db in range(2):
            slot = db % 2
            wsl = wring[:, slot * 4096: slot * 4096 + 4096]
            src = wqkv_s[:, 2 * D + db * 512: 2 * D + (db + 1) * 512]
            p.dma("sync", wsl.rearrange("p (c n) -> p c n", c=NCHLOC),
                  src.rearrange("(c p) n -> p c n", p=128),
                  reads=("wqkv_sv", EP, f"wr{slot}"), writes=(f"wr{slot}",),
                  tag=f"vw{db}")
            for jt in range(JT):
                pc = (jt % 4) * 512
                for fc in range(NCHLOC):
                    p.op("pe", lambda e, wsl=wsl, fc=fc, jt=jt, pc=pc:
                         e.matmul(pcol(pc, 512),
                                  xbf[:, fc * S + jt * 128: fc * S + jt * 128 + 128],
                                  wsl[:, fc * 512:(fc + 1) * 512],
                                  start=(fc == 0), stop=(fc == NCHLOC - 1)),
                         reads=(f"wr{slot}", xk[fc]),
                         writes=(f"ps{pc // 512}",), tag="vmm")
                vv = Vp[:, jt * 1024 + db * 512: jt * 1024 + (db + 1) * 512]
                p.op("dve", lambda e, vv=vv, pc=pc:
                     e.tensor_copy(vv, pcol(pc, 512)),
                     reads=(f"ps{pc // 512}",), writes=(f"Vp{jt}",), tag="vev")
                fire(2 * NCHLOC - 1 + db * JT + jt)
        fire(len(cslots) - 1)

        if dbg == "qkv":
            for mc in range(NCHLOC):
                p.dma("pool", out_d[b, mc * 128:(mc + 1) * 128, 0:S],
                      Qb[:, mc * S:(mc + 1) * S], reads=(f"Qb{mc}",),
                      writes=("st0",), tag="dbgq")
            info = p.emit(final_waits=[("pool", "st0")])
            nc._dbg_handles = dict(Kb=Kb, Vp=Vp)
            return nc, info

        if b == 0:
            for e_ in range(E):
                p.dma("pool", w1_s[e_], w1T_d[e_],
                      reads=("wr0", "wr1"), writes=(f"w1s{e_}",))
                p.dma("pool", w2_s[e_], w2T_d[e_],
                      reads=("wr0", "wr1"), writes=(f"w2s{e_}",))

        def dn_evac(h):
            sl = (h % 2) * IW
            for ih in range(NIH):
                p.op("dve", lambda e, sl=sl, ih=ih: e.tensor_copy(
                    concU[0:64, sl + ih * 512: sl + ih * 512 + 512],
                    pcol(OB + ih * 512, 512, 0, 64)),
                     reads=(f"ps{(OB + ih * 512) // 512}",),
                     writes=(f"cu{h % 2}",), tag="cuev")
            for ih in range(NIH):
                p.op("dve", lambda e, sl=sl, ih=ih: e.tensor_copy(
                    dnb[0:64, sl + ih * 512: sl + ih * 512 + 512],
                    pcol(ih * 512, 512, 0, 64)),
                     reads=(f"ps{ih}",), writes=(f"dn{h % 2}",), tag="dncp")

        OB = 3072
        okeys = ("ps6", "ps7")

        def norm_head(h):
            mc = h // 2
            r0 = (h % 2) * 64
            sl = (h % 2) * IW
            p.op("dve", lambda e, sl=sl: e.reciprocal(
                bcbr[0:64, 0:IW], dnb[0:64, sl:sl + IW]),
                 reads=(f"dn{h % 2}",), writes=("bcbr",), tag="drcp",
                 lowp=True)
            p.op("dve", lambda e, mc=mc, r0=r0, sl=sl:
                 e.tensor_mul(conc[r0:r0 + 64, mc * IW:(mc + 1) * IW],
                              concU[0:64, sl:sl + IW], bcbr[0:64, 0:IW]),
                 reads=(f"cu{h % 2}", "bcbr"),
                 writes=(f"cc{mc}",),
                 tag="avnrm")

        pend_ln1 = [None]

        def ln1_tail(i2p):
            iop = i2p * IW
            _ln_smalls_b(p, pcol, IW, strow, rmB, (rowb, ones_r[:, :]), "at", bccol=1024)
            for ec in range(NCHLOC):
                p.op("dve", lambda e, ec=ec, iop=iop: e.tensor_mul(
                    sqt[:, 0:IW], xbf[:, ec * S + iop: ec * S + iop + IW],
                    rmB[:, 0:IW]),
                    reads=(xk[ec], "rmBr"), writes=("sqt",), tag="nrm1")
                p.op("dve", lambda e, ec=ec, iop=iop: e.tensor_sub(
                    xbf[:, ec * S + iop: ec * S + iop + IW], sqt[:, 0:IW],
                    rmB[:, IW:2 * IW]),
                    reads=("sqt", "rmBm", xk[ec]), writes=(xk[ec],),
                    tag="nrm2")

        for i2 in range(NI2):
            io = i2 * IW
            for h in range(H):
                mc = h // 2
                r0 = (h % 2) * 64

                def emit_s(j):
                    scol = 1024 + (j % 2) * 1024
                    for ih in range(NIH):
                        p.op("pe", lambda e, mc=mc, r0=r0, j=j, ih=ih, io=io,
                             scol=scol:
                             e.matmul(pcol(scol + ih * 512, 512),
                                      Kb[r0:r0 + 64,
                                         mc * S + j * 128: mc * S + j * 128 + 128],
                                      Qb[r0:r0 + 64,
                                         mc * S + io + ih * 512:
                                         mc * S + io + ih * 512 + 512],
                                      start=True, stop=True),
                             reads=(f"Kb{mc}", f"Qb{mc}"),
                             writes=(f"ps{(scol + ih * 512) // 512}",),
                             tag="smm")

                def emit_exp(j):
                    scol = 1024 + (j % 2) * 1024
                    pslot = j % 3
                    p.op("act", lambda e, pslot=pslot, mc=mc, scol=scol:
                         e.activation(out=ptmp[:, pslot * IW:(pslot + 1) * IW],
                                      in_=pcol(scol, IW), func=AF.Exp,
                                      bias=cbias[:, mc:mc + 1], scale=0.125),
                         reads=tuple(pbanks(scol, IW)) + (f"cb{mc}",),
                         writes=(f"pt{pslot}",), tag="exp")

                def emit_avden(j):
                    pslot = j % 3
                    for ih in range(NIH):
                        p.op("pe", lambda e, j=j, ih=ih, pslot=pslot, h=h:
                             e.matmul(pcol(OB + ih * 512, 512, 0, 64),
                                      Vp[:, (j * H + h) * 64:
                                         (j * H + h) * 64 + 64],
                                      ptmp[:, pslot * IW + ih * 512:
                                           pslot * IW + ih * 512 + 512],
                                      start=(j == 0), stop=(j == JT - 1)),
                             reads=(f"Vp{j}", f"pt{pslot}"),
                             writes=(f"ps{(OB + ih * 512) // 512}",),
                             tag="avmm")
                        p.op("pe", lambda e, j=j, ih=ih, pslot=pslot:
                             e.matmul(pcol(ih * 512, 512, 0, 64),
                                      ones64[:, :],
                                      ptmp[:, pslot * IW + ih * 512:
                                           pslot * IW + ih * 512 + 512],
                                      start=(j == 0), stop=(j == JT - 1)),
                             reads=("ones64", f"pt{pslot}"),
                             writes=(f"ps{ih}",),
                             tag="dnmm")

                emit_s(0)
                emit_s(1)
                emit_exp(0)
                if h > 0:
                    norm_head(h - 1)
                elif pend_ln1[0] is not None:
                    ln1_tail(pend_ln1[0])
                    pend_ln1[0] = None
                for j in range(2, JT):
                    emit_s(j)
                    emit_exp(j - 1)
                    emit_avden(j - 2)
                emit_exp(JT - 1)
                emit_avden(JT - 2)
                emit_avden(JT - 1)
                dn_evac(h)
            norm_head(H - 1)
            if dbg == "conc":
                for fc in range(NCHLOC):
                    p.dma("pool", out_d[b, fc * 128:(fc + 1) * 128, io:io + IW],
                          conc[:, fc * IW:(fc + 1) * IW], reads=(f"cc{fc}",),
                          writes=("st0",), tag="dbgc")
                continue
            def _pstats(ec):
                stgt = sqt if ec % 2 == 0 else ptmp
                skey = "sqt" if ec % 2 == 0 else "pt0"
                for ih in range(NIH):
                    p.op("pe", lambda e, ec=ec, io=io, ih=ih: e.matmul(
                        pcol(1024 + ih * 512, 512, 0, 1), ones_c[:, :],
                        xbf[:, ec * S + io + ih * 512:
                            ec * S + io + ih * 512 + 512],
                        start=(ec == 0), stop=(ec == NCHLOC - 1)),
                        reads=("onesc", xk[ec]),
                        writes=(f"ps{2 + ih}",), tag="s1mm")
                    p.op("pe", lambda e, ih=ih, ec=ec, stgt=stgt: e.matmul(
                        pcol(2048 + ih * 512, 512, 0, 1), ones_c[:, :],
                        stgt[:, ih * 512: ih * 512 + 512],
                        start=(ec == 0), stop=(ec == NCHLOC - 1)),
                        reads=("onesc", skey),
                        writes=(f"ps{4 + ih}",), tag="s2mm")

            for ec in range(NCHLOC):
                slot = ec % 2
                pb = 0 if ec % 2 == 0 else 3072
                wsl = wring[:, slot * 4096: slot * 4096 + 1024]
                src = wout_s[:, ec * 128:(ec + 1) * 128]
                p.dma("sync", wsl.rearrange("p (c n) -> p c n", c=NCHLOC),
                      src.rearrange("(c p) n -> p c n", p=128),
                      reads=("wout_s", EP, f"wr{slot}"), writes=(f"wr{slot}",),
                      tag="woutw")
                for ih in range(NIH):
                    for fc in range(NCHLOC):
                        p.op("pe", lambda e, wsl=wsl, fc=fc, ih=ih, pb=pb:
                             e.matmul(pcol(pb + ih * 512, 512),
                                      wsl[:, fc * 128:(fc + 1) * 128],
                                      conc[:, fc * IW + ih * 512:
                                           fc * IW + ih * 512 + 512],
                                      start=(fc == 0), stop=(fc == NCHLOC - 1)),
                             reads=(f"wr{slot}", f"cc{fc}"),
                             writes=(f"ps{(pb + ih * 512) // 512}",),
                             tag="pjmm")
                p.op("dve", lambda e, ec=ec, io=io, pb=pb: e.tensor_add(
                    xbf[:, ec * S + io: ec * S + io + IW],
                    xbf[:, ec * S + io: ec * S + io + IW], pcol(pb, IW)),
                    reads=tuple(pbanks(pb, IW)) + (xk[ec],), writes=(xk[ec],),
                    tag="resid1")
                stgt = sqt if ec % 2 == 0 else ptmp
                skey = "sqt" if ec % 2 == 0 else "pt0"
                p.op("act", lambda e, ec=ec, io=io, stgt=stgt: e.activation(
                    out=stgt[:, 0:IW],
                    in_=xbf[:, ec * S + io: ec * S + io + IW],
                    func=AF.Square), reads=(xk[ec],), writes=(skey,),
                    tag="lnsq")
                if ec >= 1:
                    _pstats(ec - 1)
            _pstats(NCHLOC - 1)
            _ln_smalls_a(p, pcol, IW, D, strow, "at", s1c=1024, s2c=2048)
            if dbg == "y1":
                pend_ln1[0] = None
            elif i2 == NI2 - 1:
                ln1_tail(i2)
            else:
                pend_ln1[0] = i2

        if dbg == "conc":
            break
        if dbg in ("y1", "x1"):
            break
        NBLK = NG * 4
        Gt = outf[:, 0:NBLK * E]
        M1 = outf[:, NBLK * E: NBLK * E + NBLK]
        M2 = outf[:, NBLK * E + NBLK: NBLK * E + 2 * NBLK]
        for t in range(NBLK):
            to = t * 128
            for fc in range(NCHLOC):
                p.op("pe", lambda e, fc=fc, to=to, t=t: e.matmul(
                    pcol(t * E, E), xbf[:, fc * S + to: fc * S + to + 128],
                    gate_b[:, fc * E:(fc + 1) * E],
                    start=(fc == 0), stop=(fc == NCHLOC - 1)),
                    reads=(xk[fc], "gateb"), writes=("ps0",),
                    tag="gmm")
        p.op("act", lambda e: e.copy(out=Gt, in_=pcol(0, NBLK * E)),
             reads=("ps0",), writes=("gt",), tag="gevac")
        for t in range(NBLK):
            p.op("dve", lambda e, t=t: e.tensor_add(
                Gt[:, t * E:(t + 1) * E], Gt[:, t * E:(t + 1) * E], bbB[:, :]),
                reads=("gt", "bbB"), writes=("gt",), tag="gadd")
        p.op("act", lambda e: e.activation(out=Gt, in_=Gt, func=AF.Sigmoid),
             reads=("gt",), writes=("gt",), tag="gsig")
        cqs = [combq[:, t * E:(t + 1) * E] for t in range(NBLK)]
        cqk = [f"cq{t // 4}{t % 4}" for t in range(NBLK)]
        for t in range(NBLK):
            p.op("dve", lambda e, t=t: e.reduce_max(
                M1[:, t:t + 1], Gt[:, t * E:(t + 1) * E], axis=AX),
                reads=("gt",), writes=("gm1",), tag="gm1")
        for t in range(NBLK):
            p.op("dve", lambda e, t=t: e.tensor_scalar(
                cqs[t], Gt[:, t * E:(t + 1) * E], M1[:, t:t + 1], -1e30,
                ALU.is_ge, ALU.mult),
                reads=("gt", "gm1"), writes=(cqk[t],), tag="gmask")
        for t in range(NBLK):
            p.op("dve", lambda e, t=t: e.tensor_add(
                cqs[t], cqs[t], Gt[:, t * E:(t + 1) * E]),
                reads=(cqk[t], "gt"), writes=(cqk[t],), tag="gmadd")
        for t in range(NBLK):
            p.op("dve", lambda e, t=t: e.reduce_max(
                M2[:, t:t + 1], cqs[t], axis=AX),
                reads=(cqk[t],), writes=("gm2",), tag="gm2")
        for t in range(NBLK):
            p.op("dve", lambda e, t=t: e.tensor_scalar(
                cqs[t], Gt[:, t * E:(t + 1) * E], M2[:, t:t + 1], None,
                ALU.is_ge),
                reads=("gt", "gm2"), writes=(cqk[t],), tag="gmask2")
        p.op("dve", lambda e: e.tensor_add(M1, M1, M2),
             reads=("gm1", "gm2"), writes=("gm1",))
        p.op("dve", lambda e: e.tensor_scalar_add(M1, M1, 1e-6),
             reads=("gm1",), writes=("gm1",))
        p.op("dve", lambda e: e.reciprocal(M1, M1),
             reads=("gm1",), writes=("gm1",))
        for t in range(NBLK):
            p.op("dve", lambda e, t=t: e.scalar_tensor_tensor(
                cqs[t], Gt[:, t * E:(t + 1) * E], M1[:, t:t + 1], cqs[t],
                ALU.mult, ALU.mult),
                reads=("gt", "gm1", cqk[t]), writes=(cqk[t],), tag="gcomb")

        prefetched_e0 = (S == 2048)
        if prefetched_e0:
            assert o1 - NCHLOC * S * 2 == f1 - 16384 * 2 and f1 == o1 \
                and f2 == o2, "ws/Kb/Qb alias layout changed"
            p.dma("sync", ws0[:, :].rearrange("p (c n) -> p c n", c=NCHLOC),
                  w1_s[0].rearrange("(c p) n -> p c n", p=128),
                  reads=("w1s0",),
                  writes=("ws0",) + tuple(f"Kb{m}" for m in range(NCHLOC)),
                  tag="w1pre")
            p.dma("sync", ws1[:, :].rearrange("p (c n) -> p c n", c=16),
                  w2_s[0].rearrange("(c p) n -> p c n", p=128),
                  reads=("w2s0",),
                  writes=("ws1",) + tuple(f"Qb{m}" for m in range(NCHLOC)),
                  tag="w2pre")
        bump_epoch()
        wslots = [ws0, ws1, ws2]
        wskeys = ["ws0", "ws1", "ws2"]
        si = 0
        for gp in GP:
            for e_ in range(E):
                csl = si
                sl1 = wslots[si]
                k1 = wskeys[si]
                sl2 = wslots[(si + 1) % 3]
                k2 = wskeys[(si + 1) % 3]
                si = (si + 2) % 3
                if not (gp is GP[0] and e_ == 0 and prefetched_e0):
                    p.dma("sync",
                          sl1[:, :].rearrange("p (c n) -> p c n", c=NCHLOC),
                          w1_s[e_].rearrange("(c p) n -> p c n", p=128),
                          reads=(f"w1s{e_}", EP, k1), writes=(k1,),
                          tag=f"w1l{e_}")
                    p.dma("sync",
                          sl2[:, :].rearrange("p (c n) -> p c n", c=16),
                          w2_s[e_].rearrange("(c p) n -> p c n", p=128),
                          reads=(f"w2s{e_}", EP, k2), writes=(k2,),
                          tag=f"w2l{e_}")
                for gi, g in enumerate(gp):
                    for tt in range(4):
                        p.op("pe", lambda e, e_=e_, gi=gi, tt=tt, g=g:
                             e.transpose(
                            pcol(1024 + gi * 512 + tt * 128, 128, 0, 1),
                            combq[:, (g * 4 + tt) * E + e_:
                                  (g * 4 + tt) * E + e_ + 1],
                            id_f[:, :]),
                            reads=(f"cq{g}{tt}", "idf"),
                            writes=(f"ps{2 + gi}",), tag="ctr")
                    p.op("act", lambda e, gi=gi: e.copy(
                        out=rowb[0:1, gi * 512:(gi + 1) * 512],
                        in_=pcol(1024 + gi * 512, 512, 0, 1)),
                        reads=(f"ps{2 + gi}",), writes=("rowb_r",),
                        tag="crev")
                    p.op("pe", lambda e, gi=gi: e.matmul(
                        pcol(1024 + gi * 512, 512), ones_r[:, :],
                        rowb[0:1, gi * 512:(gi + 1) * 512],
                        start=True, stop=True),
                        reads=("onesr", "rowb_r"), writes=(f"ps{2 + gi}",),
                        tag="cbcm")
                    p.op("dve", lambda e, gi=gi: e.tensor_copy(
                        combB[:, gi * 512:(gi + 1) * 512],
                        pcol(1024 + gi * 512, 512)),
                        reads=(f"ps{2 + gi}",), writes=(f"cB{gi}",),
                        tag="cbev")
                for gi, g in enumerate(gp):
                    go = g * 512
                    for ehc in range(16):
                        pc = (ehc % 3) * 512
                        for fc in range(NCHLOC):
                            p.op("pe", lambda e, sl1=sl1, fc=fc, ehc=ehc,
                                 go=go, pc=pc: e.matmul(
                                     pcol(pc, 512),
                                     sl1[:, fc * 2048 + ehc * 128:
                                         fc * 2048 + ehc * 128 + 128],
                                     xbf[:, fc * S + go: fc * S + go + 512],
                                     start=(fc == 0), stop=(fc == NCHLOC - 1)),
                                 reads=(k1, xk[fc]),
                                 writes=(f"ps{pc // 512}",), tag="amm")
                        hts = ehc % 2
                        p.op("act", lambda e, hts=hts, pc=pc: e.activation(
                            out=htmp[:, hts * 512:(hts + 1) * 512],
                            in_=pcol(pc, 512), func=AF.Silu),
                            reads=(f"ps{pc // 512}",), writes=(f"ht{hts}",),
                            tag="silu")
                        p.op("dve", lambda e, hts=hts, ehc=ehc, gi=gi:
                             e.tensor_mul(hp[:, ehc * 512:(ehc + 1) * 512],
                                          htmp[:, hts * 512:(hts + 1) * 512],
                                          combB[:, gi * 512:(gi + 1) * 512]),
                             reads=(f"ht{hts}", f"cB{gi}"),
                             writes=(f"hp{ehc}",), tag="hsc")
                    for dh in range(2):
                        base = dh * 2048
                        for ehc in range(16):
                            for dc in range(4):
                                d_ = dh * 4 + dc
                                p.op("pe", lambda e, sl2=sl2, ehc=ehc, d_=d_,
                                     base=base, dc=dc: e.matmul(
                                         pcol(base + dc * 512, 512),
                                         sl2[:, ehc * 1024 + d_ * 128:
                                             ehc * 1024 + d_ * 128 + 128],
                                         hp[:, ehc * 512:(ehc + 1) * 512],
                                         start=(ehc == 0), stop=(ehc == 15)),
                                     reads=(k2, f"hp{ehc}"),
                                     writes=(f"ps{(base + dc * 512) // 512}",),
                                     tag="bmm")
                        for dc in range(4):
                            d_ = dh * 4 + dc
                            ackey = f"ac{gi}{d_}"
                            asl = acc[:, (gi * NCHLOC + d_) * 512:
                                      (gi * NCHLOC + d_ + 1) * 512]
                            if e_ == 0:
                                p.op("dve", lambda e, asl=asl, base=base,
                                     dc=dc: e.tensor_copy(
                                         asl, pcol(base + dc * 512, 512)),
                                     reads=(f"ps{(base + dc * 512) // 512}",),
                                     writes=(ackey,), tag="acc0")
                            else:
                                p.op("dve", lambda e, asl=asl, base=base,
                                     dc=dc: e.tensor_add(
                                         asl, asl, pcol(base + dc * 512, 512)),
                                     reads=(f"ps{(base + dc * 512) // 512}",
                                            ackey),
                                     writes=(ackey,), tag="acc")
            for gi, g in enumerate(gp):
                go = g * 512
                for ec in range(NCHLOC):
                    ackey = f"ac{gi}{ec}"
                    asl = acc[:, (gi * NCHLOC + ec) * 512:
                              (gi * NCHLOC + ec + 1) * 512]
                    hsl = hp[:, (gi * NCHLOC + ec) * 512:
                             (gi * NCHLOC + ec + 1) * 512]
                    hkey = f"hp{gi * NCHLOC + ec}"
                    p.op("dve", lambda e, asl=asl, hsl=hsl, ec=ec, go=go:
                         e.tensor_add(hsl,
                                      xbf[:, ec * S + go: ec * S + go + 512],
                                      asl),
                         reads=(ackey, xk[ec]), writes=(hkey,), tag="y2")
                    ho = (ec % 2) * 512
                    hok = f"ht{ec % 2}"
                    p.op("act", lambda e, hsl=hsl, ho=ho: e.activation(
                        out=htmp[:, ho:ho + 512], in_=hsl, func=AF.Square),
                        reads=(hkey,), writes=(hok,), tag="l2sq")
                    p.op("pe", lambda e, ec=ec, gi=gi, hsl=hsl: e.matmul(
                        pcol(2048 + gi * 512, 512, 0, 1), ones_c[:, :], hsl,
                        start=(ec == 0), stop=(ec == NCHLOC - 1)),
                        reads=("onesc", hkey), writes=(f"ps{4 + gi}",),
                        tag="t1mm")
                    p.op("pe", lambda e, ec=ec, gi=gi, ho=ho: e.matmul(
                        pcol(3072 + gi * 512, 512, 0, 1), ones_c[:, :],
                        htmp[:, ho:ho + 512],
                        start=(ec == 0), stop=(ec == NCHLOC - 1)),
                        reads=("onesc", hok), writes=(f"ps{6 + gi}",),
                        tag="t2mm")
            WW = 512 * len(gp)
            _ln_smalls(p, pcol, WW, D, strow, combB,
                       (rowb, ones_r[:, :]), "ff",
                       s1c=2048, s2c=3072, bccol=2048,
                       roff=1024, moff=2048, rkey="cB2", mkey="cB3")
            for gi, g in enumerate(gp):
                go = g * 512
                for ec in range(NCHLOC):
                    osl = ec % 2
                    hsl = hp[:, (gi * NCHLOC + ec) * 512:
                             (gi * NCHLOC + ec + 1) * 512]
                    hkey = f"hp{gi * NCHLOC + ec}"
                    p.op("dve", lambda e, hsl=hsl, gi=gi: e.tensor_mul(
                        htmp[:, 512:1024], hsl,
                        combB[:, 1024 + gi * 512: 1024 + (gi + 1) * 512]),
                        reads=(hkey, "cB2"), writes=("ht1",), tag="o1")
                    p.op("dve", lambda e, osl=osl, gi=gi: e.tensor_sub(
                        outf[:, osl * 512:(osl + 1) * 512], htmp[:, 512:1024],
                        combB[:, 2048 + gi * 512: 2048 + (gi + 1) * 512]),
                        reads=("ht1", "cB3"), writes=(f"of{osl}",), tag="o2")
                    p.dma("pool",
                          out_d[b, ec * 128:(ec + 1) * 128, go:go + 512],
                          outf[:, osl * 512:(osl + 1) * 512],
                          reads=(f"of{osl}",), writes=(f"st{osl}",),
                          tag="store")

    if dbg in ("conc", "y1", "x1"):
        if dbg in ("y1", "x1"):
            for c in range(NCHLOC):
                p.dma("pool", out_d[0, c * 128:(c + 1) * 128, 0:S],
                      xbf[:, c * S:(c + 1) * S], reads=(xk[c],),
                      writes=("st0",), tag="dbgx")
        info = p.emit(final_waits=[("pool", "st0")])
        return nc, info
    info = p.emit(final_waits=[("pool", "st0"), ("pool", "st1")])
    return nc, info


def _ln_smalls_a(p, pcol, W, Dm, strow, tagp, s1c, s2c):
    ks1 = [f"ps{k}" for k in range(s1c // 512, (s1c + W + 511) // 512)]
    ks2 = [f"ps{k}" for k in range(s2c // 512, (s2c + W + 511) // 512)]
    p.op("act", lambda e: e.mul(out=strow[0:1, 0:W], in_=pcol(s1c, W, 0, 1),
                                mul=1.0 / Dm),
         reads=tuple(ks1), writes=("st_m",), tag=tagp + "mean")
    p.op("act", lambda e: e.mul(out=strow[0:1, 2 * W:3 * W],
                                in_=pcol(s2c, W, 0, 1), mul=1.0 / Dm),
         reads=tuple(ks2), writes=("st_v",), tag=tagp + "ey2")


def _ln_smalls_b(p, pcol, W, strow, rmB, onesrf, tagp, bccol,
                 roff=0, moff=None, rkey="rmBr", mkey="rmBm"):
    IWtot = rmB.shape[1] // 2 if moff is None else moff
    p.op("dve", lambda e: e.tensor_mul(strow[0:1, W:2 * W], strow[0:1, 0:W],
                                       strow[0:1, 0:W]),
         reads=("st_m",), writes=("st_t",), tag=tagp + "m2")
    p.op("dve", lambda e: e.tensor_sub(strow[0:1, W:2 * W],
                                       strow[0:1, 2 * W:3 * W],
                                       strow[0:1, W:2 * W]),
         reads=("st_t", "st_v"), writes=("st_t",), tag=tagp + "var")
    p.op("dve", lambda e: e.tensor_scalar_add(strow[0:1, W:2 * W],
                                              strow[0:1, W:2 * W], 1e-5),
         reads=("st_t",), writes=("st_t",), tag=tagp + "eps")
    p.op("act", lambda e: e.activation(out=strow[0:1, W:2 * W],
                                       in_=strow[0:1, W:2 * W], func=AF.Ln),
         reads=("st_t",), writes=("st_t",), tag=tagp + "ln")
    p.op("act", lambda e: e.activation(out=strow[0:1, W:2 * W],
                                       in_=strow[0:1, W:2 * W], func=AF.Exp,
                                       scale=-0.5),
         reads=("st_t",), writes=("st_t",), tag=tagp + "exp")
    p.op("dve", lambda e: e.tensor_mul(strow[0:1, 2 * W:3 * W],
                                       strow[0:1, 0:W], strow[0:1, W:2 * W]),
         reads=("st_m", "st_t"), writes=("st_v",), tag=tagp + "mr")
    rowb, onesb = onesrf
    nih = W // 512
    p.op("act", lambda e: e.copy(out=rowb[0:1, 0:W], in_=strow[0:1, W:2 * W]),
         reads=("st_t",), writes=("rowb_r",), tag=tagp + "rcv")
    p.op("act", lambda e: e.copy(out=rowb[0:1, W:2 * W],
                                 in_=strow[0:1, 2 * W:3 * W]),
         reads=("st_v",), writes=("rowb_m",), tag=tagp + "mcv")
    for ih in range(nih):
        p.op("pe", lambda e, ih=ih: e.matmul(
            pcol(bccol + ih * 512, 512), onesb,
            rowb[0:1, ih * 512: ih * 512 + 512],
            start=True, stop=True),
            reads=("onesr", "rowb_r"),
            writes=(f"ps{(bccol + ih * 512) // 512}",), tag=tagp + "rbc")
    p.op("dve", lambda e: e.tensor_copy(rmB[:, roff:roff + W],
                                        pcol(bccol, W)),
         reads=tuple(f"ps{k}" for k in range(bccol // 512,
                                             (bccol + W + 511) // 512)),
         writes=(rkey,), tag=tagp + "rbe")
    for ih in range(nih):
        p.op("pe", lambda e, ih=ih: e.matmul(
            pcol(bccol + ih * 512, 512), onesb,
            rowb[0:1, W + ih * 512: W + ih * 512 + 512],
            start=True, stop=True),
            reads=("onesr", "rowb_m"),
            writes=(f"ps{(bccol + ih * 512) // 512}",), tag=tagp + "mbc")
    p.op("dve", lambda e: e.tensor_copy(rmB[:, IWtot:IWtot + W],
                                        pcol(bccol, W)),
         reads=tuple(f"ps{k}" for k in range(bccol // 512,
                                             (bccol + W + 511) // 512)),
         writes=(mkey,), tag=tagp + "mbe")


def _ln_smalls(p, pcol, W, Dm, strow, rmB, onesrf, tagp, s1c, s2c, bccol,
               roff=0, moff=None, rkey="rmBr", mkey="rmBm"):
    _ln_smalls_a(p, pcol, W, Dm, strow, tagp, s1c, s2c)
    _ln_smalls_b(p, pcol, W, strow, rmB, onesrf, tagp, bccol,
                 roff=roff, moff=moff, rkey=rkey, mkey=mkey)


def _top2(p, gs, cq, gm, outkey):
    m1 = gm[:, 0:1]
    m2 = gm[:, 1:2]
    p.op("dve", lambda e: e.reduce_max(m1, gs, axis=AX),
         reads=("gt",), writes=("g_m1",))
    p.op("dve", lambda e: e.tensor_scalar(cq, gs, m1, -1e30,
                                          ALU.is_ge, ALU.mult),
         reads=("gt", "g_m1"), writes=(outkey,))
    p.op("dve", lambda e: e.tensor_add(cq, cq, gs),
         reads=(outkey, "gt"), writes=(outkey,))
    p.op("dve", lambda e: e.reduce_max(m2, cq, axis=AX),
         reads=(outkey,), writes=("g_m2",))
    p.op("dve", lambda e: e.tensor_scalar(cq, gs, m2, None,
                                          ALU.is_ge),
         reads=("gt", "g_m2"), writes=(outkey,))
    p.op("dve", lambda e: e.tensor_add(m1, m1, m2),
         reads=("g_m1", "g_m2"), writes=("g_m1",))
    p.op("dve", lambda e: e.tensor_scalar_add(m1, m1, 1e-6),
         reads=("g_m1",), writes=("g_m1",))
    p.op("dve", lambda e: e.reciprocal(m1, m1),
         reads=("g_m1",), writes=("g_m1",))
    p.op("dve", lambda e: e.scalar_tensor_tensor(cq, gs, m1, cq,
                                                 ALU.mult, ALU.mult),
         reads=("gt", "g_m1", outkey), writes=(outkey,))


_built = {}


def _get_nc(S, NB):
    key = (S, NB)
    if key not in _built:
        _built[key] = build_nc(S, NB)
    return _built[key]


def kernel(L0_attn_x, L0_attn_w_qkv, L0_attn_w_out, L0_ffn_gate_w,
           L0_ffn_balance_bias, L0_ffn_w1, L0_ffn_w2):
    x = np.asarray(L0_attn_x, dtype=np.float32)
    B, S, D_ = x.shape
    NB = B // 8
    nc, info = _get_nc(S, NB)

    x_fm = np.ascontiguousarray(x.transpose(0, 2, 1))
    wqkvT = np.ascontiguousarray(np.asarray(L0_attn_w_qkv, np.float32).T)
    woutT = np.ascontiguousarray(np.asarray(L0_attn_w_out, np.float32).T)
    gateT = np.ascontiguousarray(np.asarray(L0_ffn_gate_w, np.float32).T)
    bb = np.ascontiguousarray(
        np.asarray(L0_ffn_balance_bias, np.float32).reshape(1, E))
    w1T = np.ascontiguousarray(
        np.asarray(L0_ffn_w1, np.float32).transpose(0, 2, 1))
    w2T = np.ascontiguousarray(
        np.asarray(L0_ffn_w2, np.float32).transpose(0, 2, 1))

    in_maps = []
    for c in range(8):
        in_maps.append({
            "x": np.ascontiguousarray(x_fm[c * NB:(c + 1) * NB]),
            "wqkvT": wqkvT, "woutT": woutT, "gateT": gateT, "bb": bb,
            "w1T": w1T, "w2T": w2T,
        })
    res = run_bass_kernel_spmd(nc, in_maps, core_ids=list(range(8)))
    outs = [res.results[c]["out"] for c in range(8)]
    full = np.concatenate(outs, axis=0)
    return np.ascontiguousarray(full.transpose(0, 2, 1)).astype(np.float32)

